# version 85
# baseline (speedup 1.0000x reference)
import numpy as np
import concourse.bass as bass
import concourse.mybir as mybir
from concourse.bass_utils import run_bass_kernel_spmd
from contextlib import ExitStack

F32 = mybir.dt.float32
BF16 = mybir.dt.bfloat16
AF = mybir.ActivationFunctionType
ALU = mybir.AluOpType

D = 1024
T = 4096
NS = 256
TILE = 512
NT = T // TILE
DC = D // 128
NC = NS // 128
EPS = 1e-5

_NC_CACHE = {}


def build_nc(debug=False):
    nc = bass.Bass("TRN2", target_bir_lowering=False, dynamic_dma_scratch_size=1024)

    xT = nc.dram_tensor("xT", [D, T], F32, kind="ExternalInput")
    oT = nc.dram_tensor("oT", [D, T], F32, kind="ExternalOutput")
    dw = {}
    for name, shp in [("Wa", [D, NS]), ("Wb", [D, NS]), ("Wc", [D, NS]),
                      ("Wd", [D, D]), ("Wy", [NS, D]),
                      ("w1", [D, D]), ("w2", [D, 512]), ("w3", [512, 256]), ("w4", [256, 1]),
                      ("g1", [D, 512]), ("g2", [512, 256]), ("g3", [256, 1])]:
        dw[name] = nc.dram_tensor(name, shp, F32, kind="ExternalInput")
    bspec = [("ba2", [128, 2]), ("bb2", [128, 2]), ("bc2", [128, 2]),
             ("bd8", [128, 8]), ("by8", [128, 8]),
             ("b1c", [128, 8]), ("b2c", [128, 4]), ("b3c", [128, 2]), ("b4", [1, 1]),
             ("gb1", [128, 4]), ("gb2", [128, 2]), ("gb3", [1, 1]), ("ar", [1, 1])]
    db = {name: nc.dram_tensor(name, shp, F32, kind="ExternalInput") for name, shp in bspec}

    ctx = ExitStack()
    e = ctx.enter_context

    WaB = e(nc.sbuf_tensor("WaB", [128, 8 * 256], BF16))
    WbB = e(nc.sbuf_tensor("WbB", [128, 8 * 256], BF16))
    WcB = e(nc.sbuf_tensor("WcB", [128, 8 * 256], BF16))
    WdB = e(nc.sbuf_tensor("WdB", [128, 8 * 1024], BF16))
    WyB = e(nc.sbuf_tensor("WyB", [128, 2 * 1024], BF16))
    W1B = e(nc.sbuf_tensor("W1B", [128, 8 * 1024], BF16))
    W2B = e(nc.sbuf_tensor("W2B", [128, 8 * 512], BF16))
    W3B = e(nc.sbuf_tensor("W3B", [128, 4 * 256], BF16))
    W4B = e(nc.sbuf_tensor("W4B", [128, 2 * 1], BF16))
    bt = {name: e(nc.sbuf_tensor("t_" + name, shp, F32)) for name, shp in bspec}
    ba_h = e(nc.sbuf_tensor("ba_h", [128, 2], F32))
    b4h = e(nc.sbuf_tensor("b4h", [1, 1], F32))
    g3h = e(nc.sbuf_tensor("g3h", [1, 1], F32))
    byd = e(nc.sbuf_tensor("byd", [128, 8], F32))
    onesr = e(nc.sbuf_tensor("onesr", [65, 128], F32))
    onescb = e(nc.sbuf_tensor("onescb", [128, 1], BF16))
    rmrp = e(nc.sbuf_tensor("rmrp", [128, NT * 8], F32))
    xs_store = e(nc.sbuf_tensor("xs_store", [128, DC * NT], F32))
    mrs = e(nc.sbuf_tensor("mrs", [1, NT], F32))
    mrsR = e(nc.sbuf_tensor("mrsR", [1, 1], F32))
    mrsb = e(nc.sbuf_tensor("mrsb", [128, 1], F32))
    xnmf = e(nc.sbuf_tensor("xnmf", [128, DC], F32))
    jr8 = e(nc.sbuf_tensor("jr8", [128, 8], F32))
    hg1f = e(nc.sbuf_tensor("hg1f", [128, 4], F32))
    hg2f = e(nc.sbuf_tensor("hg2f", [128, 2], F32))
    th3 = e(nc.sbuf_tensor("th3", [1, 1], F32))
    C0 = e(nc.sbuf_tensor("C0", [1, 1], F32))
    C1 = e(nc.sbuf_tensor("C1", [1, 1], F32))
    th4row = e(nc.sbuf_tensor("th4row", [65, 2 * TILE], F32))
    cbrow = e(nc.sbuf_tensor("cbrow", [1, TILE], F32))
    r_row = e(nc.sbuf_tensor("r_row", [1, TILE], F32))
    mr_row = e(nc.sbuf_tensor("mr_row", [1, TILE], F32))
    carry = e(nc.sbuf_tensor("carry", [128, NC], F32))
    eps_t = e(nc.sbuf_tensor("eps_t", [1, 1], F32))
    p0rows = e(nc.sbuf_tensor("p0rows", [65, 4 * TILE], F32))

    xbuf = [e(nc.sbuf_tensor(f"xbuf{p}", [128, DC * TILE], F32)) for p in range(2)]
    xnb = [e(nc.sbuf_tensor(f"xnb{p}", [128, DC * TILE], BF16)) for p in range(2)]
    castb = e(nc.sbuf_tensor("castb", [128, DC * TILE], BF16))
    state = [e(nc.sbuf_tensor(f"state{p}", [128, DC * TILE], F32)) for p in range(2)]
    gbuf = e(nc.sbuf_tensor("gbuf", [128, DC * TILE], F32))
    h1b = e(nc.sbuf_tensor("h1b", [128, 8 * TILE], BF16))
    h2b = e(nc.sbuf_tensor("h2b", [128, 4 * TILE], BF16))
    tw0 = e(nc.sbuf_tensor("tw0", [128, TILE], F32))
    tw1 = e(nc.sbuf_tensor("tw1", [128, TILE], F32))
    thT = e(nc.sbuf_tensor("thT", [128, NC * TILE], F32))
    aT = e(nc.sbuf_tensor("aT", [128, NC * TILE], F32))
    buT = e(nc.sbuf_tensor("buT", [128, NC * TILE], F32))
    cT = e(nc.sbuf_tensor("cT", [128, NC * TILE], F32))
    csT = e(nc.sbuf_tensor("csT", [128, NC * TILE], BF16))
    g3B = e(nc.sbuf_tensor("g3B", [128, 2], BF16))
    xnmB = e(nc.sbuf_tensor("xnmB", [128, DC], BF16))
    hg1b = e(nc.sbuf_tensor("hg1b", [128, 4], BF16))
    hg2b = e(nc.sbuf_tensor("hg2b", [128, 2], BF16))
    rb_sb = e(nc.sbuf_tensor("rb_sb", [128, TILE], F32))
    mrb_sb = e(nc.sbuf_tensor("mrb_sb", [128, TILE], F32))

    rem = nc.sbuf_bytes_remaining
    assert rem >= 0, f"SBUF over budget by {-rem} bytes"
    import sys
    print(f"[build] sbuf_bytes_remaining={rem}", file=sys.stderr)

    PB = [e(nc.psum_tensor(f"PB{i}", [128, TILE], F32)) for i in range(8)]
    DCLS = ["bias", "wst0", "wst1", "wst2", "wst3", "x0", "x1", "x0h", "x1h",
            "pack", "out", "dbg"]
    sem = {k: e(nc.semaphore("sem_" + k)) for k in ["pe", "act", "dve", "pool"] + DCLS}

    dbg_tensors = {}

    steps = []
    count = {k: 0 for k in ["pe", "act", "dve", "pool"] + DCLS}
    last_w = {}
    readers = {}

    def step(eng, fn, reads=(), writes=(), n_dma=0, strict=False, dcls=None,
             lazy_reads=()):
        deps = set()
        for r in reads:
            if r in last_w:
                deps.add(last_w[r])
        for r in writes:
            deps |= readers.get(r, set())
            if r in last_w:
                deps.add(last_w[r])
        sid = len(steps)
        is_dma = n_dma > 0
        key = dcls if is_dma else eng
        if is_dma:
            count[key] += 16 * n_dma
        else:
            count[key] += 1
        steps.append(dict(eng=eng, fn=fn, deps=deps, val=count[key], strict=strict,
                          key=key, is_dma=is_dma))
        for r in reads:
            readers.setdefault(r, set()).add(sid)
        for r in lazy_reads:
            readers.setdefault(r, set()).add(sid)
        for r in writes:
            last_w[r] = sid
            readers[r] = {sid}
        return sid

    def stchunk(pi, c):
        return f"st{pi}c{c}"

    def dbg(name, ap, reads):
        if not debug:
            return
        shp = [ap.shape[0], int(np.prod(ap.shape[1:]))]
        dt_ = nc.dram_tensor("dbg_" + name, shp, ap.dtype, kind="ExternalOutput")
        dbg_tensors[name] = dt_
        def f(eng, dsem, dt_=dt_, ap=ap):
            return eng.dma_start(dt_[:, :], ap).then_inc(dsem, 16)
        step("dma", f, reads=tuple(reads), n_dma=1, dcls="dbg")

    def consts_prep(eng):
        eng.memset(onesr[:, :], 1.0)
        eng.memset(eps_t[:, :], EPS)
        eng.memset(carry[:, :], 0.0)
        return eng.memset(onescb[:, :], 1.0)
    step("dve", consts_prep, writes=("onesr", "eps_t", "onescb", "carry"))

    def fbias(eng, dsem):
        ins = None
        for name in db:
            ins = eng.dma_start(bt[name][:, :], db[name][:, :]).then_inc(dsem, 16)
        return ins

    def bias_prep(eng):
        eng.tensor_scalar(ba_h[:, :], bt["ba2"][:, :], 0.5, None, ALU.mult)
        eng.tensor_scalar(b4h[:, :], bt["b4"][:, :], 0.5, None, ALU.mult)
        eng.tensor_scalar(g3h[:, :], bt["gb3"][:, :], 0.5, None, ALU.mult)
        return eng.tensor_tensor(byd[:, :], bt["bd8"][:, :], bt["by8"][:, :], ALU.add)

    wspec = [("w1", W1B, 8, 1024), ("w2", W2B, 8, 512), ("w3", W3B, 4, 256), ("w4", W4B, 2, 1),
             ("Wa", WaB, 8, 256), ("Wb", WbB, 8, 256), ("Wc", WcB, 8, 256),
             ("Wd", WdB, 8, 1024), ("Wy", WyB, 2, 1024)]
    bigbufs = [(gbuf, 0, tuple(f"wstg0c{c}" for c in range(4)),
                tuple(f"g{c}" for c in range(4))),
               (gbuf, 2048, tuple(f"wstg1c{c}" for c in range(4)),
                tuple(f"g{c}" for c in range(4, 8)))]
    wpieces = []
    for name, wb, nk, M in wspec:
        nh = max(1, M // TILE)
        for k in range(nk):
            for h in range(nh):
                wpieces.append((name, wb, M, k, h * TILE, min(M, h * TILE + TILE),
                                ("W" + name,)))
    for k in range(8):
        wpieces.append(("g1", castb, 512, k, 0, 512, (f"castb{k}",)))
    for k in range(4):
        wpieces.append(("g2", h2b, 256, k, 0, 256, ("h2b",)))
    for k in range(2):
        wpieces.append(("g3", g3B, 1, k, 0, 1, ("g3B",)))
    bigjobs = []
    bi = 0
    while wpieces:
        grp = [wpieces.pop(0)]
        while wpieces and len(grp) < 4 and wpieces[0][0] == grp[0][0]:
            grp.append(wpieces.pop(0))
        bigjobs.append((grp, bi % 2))
        bi += 1

    pend_cv = []
    wround = [0]

    def emit_cvs(cvs):
        for eng_name, fcp, bres, wres in cvs:
            step(eng_name, fcp, reads=bres, writes=wres)

    def drain_cv(njobs, force=False):
        while pend_cv and njobs > 0:
            if not force and pend_cv[0][2] > wround[0] - 2:
                break
            emit_cvs(pend_cv.pop(0)[3])
            njobs -= 1

    allow_g = [False]

    def drain_w(n):
        if n <= 0:
            return
        wround[0] += 1
        drain_cv(n)
        while bigjobs and n > 0:
            if not allow_g[0] and bigjobs[0][0][0][0] in ("g1", "g2", "g3"):
                break
            grp, p = bigjobs.pop(0)
            for _i, ent in enumerate(pend_cv):
                if ent[0] == p:
                    emit_cvs(ent[3])
                    pend_cv.pop(_i)
                    break
            buf, boff, bres, gres = bigbufs[p]
            name = grp[0][0]
            d = dw[name]
            M = grp[0][2]
            npc = len(grp)
            k0, w00 = grp[0][3], grp[0][4]
            pw = grp[0][5] - grp[0][4]
            if M <= TILE:
                def fl(eng, dsem, d=d, buf=buf, boff=boff, k0=k0, npc=npc, pw=pw, M=M):
                    srcap = bass.AP(d, k0 * 128 * M, [[M, 128], [128 * M, npc], [1, pw]])
                    dstap = bass.AP(buf, boff, [[DC * TILE, 128], [TILE, npc], [1, pw]])
                    return eng.dma_start(dstap, srcap,
                                         allow_slow_non_contiguous=(pw < 16)).then_inc(dsem, 16)
            else:
                assert npc == 4 and w00 == 0 and M == 2 * TILE
                def fl(eng, dsem, d=d, buf=buf, boff=boff, k0=k0, M=M):
                    srcap = bass.AP(d, k0 * 128 * M, [[M, 128], [128 * M, 2], [1, M]])
                    dstap = bass.AP(buf, boff, [[DC * TILE, 128], [M, 2], [1, M]])
                    return eng.dma_start(dstap, srcap).then_inc(dsem, 16)
            nd = 1
            step("dma", fl, writes=bres + gres, n_dma=nd, dcls=f"wst{p}")
            cvs = []
            for i, (nm, wb, M_, kk_, w0_, w1_, wres) in enumerate(grp):
                if nm in ("g1", "g2", "g3", "w1", "w2", "Wd", "Wy"):
                    eng_name = "act"
                else:
                    eng_name = "pool"
                if eng_name == "act":
                    def fcp(eng, wb=wb, M_=M_, kk_=kk_, w0_=w0_, w1_=w1_, buf=buf, boff=boff, i=i):
                        return eng.activation(wb[:, kk_ * M_ + w0_:kk_ * M_ + w1_],
                                              buf[:, boff + i * TILE:boff + i * TILE + (w1_ - w0_)],
                                              AF.Identity)
                else:
                    def fcp(eng, wb=wb, M_=M_, kk_=kk_, w0_=w0_, w1_=w1_, buf=buf, boff=boff, i=i):
                        return eng.tensor_copy(wb[:, kk_ * M_ + w0_:kk_ * M_ + w1_],
                                               buf[:, boff + i * TILE:boff + i * TILE + (w1_ - w0_)])
                cvs.append((eng_name, fcp, bres + gres, wres))
            pend_cv.append((p, name, wround[0], cvs))
            n -= 1

    def need_w(wname):
        while any(grp[0][0] == wname for grp, _p in bigjobs):
            drain_w(1)
        for ent in [ent for ent in list(pend_cv) if ent[1] == wname]:
            pend_cv.remove(ent)
            emit_cvs(ent[3])

    def load_x(j):
        pi = j % 2
        for h in range(2):
            def f(eng, dsem, j=j, h=h):
                src = bass.AP(xT, j * TILE + h * 4 * 128 * T,
                              [[T, 128], [128 * T, DC // 2], [1, TILE]])
                return eng.dma_start(xbuf[pi][:, h * 2048:(h + 1) * 2048],
                                     src).then_inc(dsem, 16)
            step("dma", f, writes=(f"xbuf{pi}" + ("lo", "hi")[h],),
                 n_dma=1, dcls=f"x{pi}" + ("", "h")[h])

    def xr(pi):
        return (f"xbuf{pi}lo", f"xbuf{pi}hi")

    CASTR = tuple(f"castb{c}" for c in range(8))

    def p0bufs(q):
        bx = {0: xnb[0], 1: xnb[1]}
        xres = {0: ("xnb0lo", "xnb0hi"), 1: ("xnb1lo", "xnb1hi")}
        bq = {0: castb, 1: h1b}
        qres = {0: CASTR, 1: ("h1b",)}
        if q % 2:
            return (bq, qres, bx, xres)
        return (bx, xres, bq, qres)

    def p0slots(q):
        a = (q % 2)
        pp = 64 * a
        P = {h: th4row[pp:pp + 1, h * TILE:(h + 1) * TILE] for h in range(2)}
        Q = {h: p0rows[pp:pp + 1, h * TILE:(h + 1) * TILE] for h in range(2)}
        M = {h: p0rows[pp:pp + 1, (2 + h) * TILE:(3 + h) * TILE] for h in range(2)}
        bank = {0: 4 - 2 * a, 1: 5 - 2 * a}
        bcb = {0: 6 * (1 - a), 1: 6 * (1 - a) + 1}
        sfx = str(a)
        return P, Q, M, bank, bcb, sfx

    def emit_p0_stage(q):
        bufx, xres, bufq, qres = p0bufs(q)
        for h in range(2):
            for hf in range(2):
                def fbx(eng, h=h, hf=hf, bufx=bufx):
                    return eng.tensor_copy(bufx[h][:, hf * 2048:(hf + 1) * 2048],
                                           xbuf[h][:, hf * 2048:(hf + 1) * 2048])
                if xres[h] is CASTR:
                    wr = (CASTR[hf * 4:hf * 4 + 4])
                else:
                    wr = (xres[h][hf],) if len(xres[h]) == 2 else xres[h]
                step("dve", fbx, reads=(xr(h)[hf],), writes=wr)
        for h in range(2):
            def fsq(eng, h=h, bufx=bufx, bufq=bufq):
                return eng.activation(bufq[h][:, :], bufx[h][:, :], AF.Square)
            step("act", fsq, reads=xres[h], writes=qres[h])

    def emit_p0_pair(q, drains=0):
        e_, o_ = 2 * q, 2 * q + 1
        bufx, xres, bufq, qres = p0bufs(q)
        P, Q, M, bank, bcb, sfx = p0slots(q)
        for h in range(2):
            for hf in range(2):
                def fred(eng, h=h, hf=hf):
                    ins = None
                    for c in range(hf * 4, hf * 4 + 4):
                        ins = eng.matmul(PB[bank[h]][0:1, :], onescb[:, :],
                                         bufx[h][:, c * TILE:(c + 1) * TILE],
                                         start=(c == 0), stop=(c == DC - 1))
                    return ins
                rr = (xres[h][hf],) if len(xres[h]) == 2 else xres[h]
                step("pe", fred, reads=rr + ("onescb",), writes=(f"pb{bank[h]}",))
        for h in range(2):
            def fred2(eng, h=h):
                ins = None
                for c in range(DC):
                    ins = eng.matmul(PB[bank[h]][32:33, :], onescb[:, :],
                                     bufq[h][:, c * TILE:(c + 1) * TILE],
                                     start=(c == 0), stop=(c == DC - 1))
                return ins
            step("pe", fred2, reads=qres[h] + ("onescb",), writes=(f"pb{bank[h]}",))
        drain_w(drains)
        for h in range(2):
            def fzc(eng, h=h):
                return eng.tensor_scalar(Q[h], PB[bank[h]][0:1, :], 1.0, 0.0,
                                         ALU.mult, ALU.add)
            step("dve", fzc, reads=(f"pb{bank[h]}",), writes=(f"p0Q{sfx}{h}",))
        for h in range(2):
            def fz(eng, h=h):
                return eng.scalar_tensor_tensor(P[h], Q[h], -1.0 / 1024.0,
                                                Q[h], ALU.mult, ALU.mult)
            step("dve", fz, reads=(f"p0Q{sfx}{h}",),
                 writes=(f"p0P{sfx}{h}", "th4row"), strict=True)
        for h in range(2):
            def fq(eng, h=h):
                return eng.scalar_tensor_tensor(P[h], P[h], 1.0,
                                                PB[bank[h]][32:33, :],
                                                ALU.mult, ALU.add)
            step("dve", fq, reads=(f"pb{bank[h]}", f"p0P{sfx}{h}"),
                 writes=(f"p0P{sfx}{h}", "th4row"), strict=True)
        for h in range(2):
            def fln(eng, h=h):
                return eng.activation(Q[h], P[h], AF.Ln,
                                      bias=eps_t[0:1, 0:1], scale=1.0 / 1024.0)
            step("act", fln, reads=(f"p0P{sfx}{h}", "eps_t"),
                 writes=(f"p0Q{sfx}{h}",))
        for h in range(2):
            def frr(eng, h=h):
                return eng.activation(P[h], Q[h], AF.Exp, scale=-0.5)
            step("act", frr, reads=(f"p0Q{sfx}{h}",),
                 writes=(f"p0P{sfx}{h}", "th4row"), strict=True)
        for h in range(2):
            def fmr(eng, h=h, j=(e_, o_)[h]):
                return eng.scalar_tensor_tensor(M[h], PB[bank[h]][0:1, :],
                                                1.0 / 1024.0, P[h],
                                                ALU.mult, ALU.mult,
                                                accum_out=mrs[0:1, j:j + 1])
            step("dve", fmr, reads=(f"pb{bank[h]}", f"p0P{sfx}{h}"),
                 writes=(f"p0M{sfx}{h}", "mrs0"))
        drain_w(drains)
        pp_ = 64 * (q % 2)
        for h in range(2):
            def fbc(eng, h=h, pp_=pp_):
                return eng.matmul(PB[bcb[h]][:, :], onesr[pp_:pp_ + 1, :], P[h],
                                  start=True, stop=True)
            step("pe", fbc, reads=(f"p0P{sfx}{h}", "th4row", "onesr"),
                 writes=(f"pb{bcb[h]}",))
        for h in range(2):
            def fxsum(eng, h=h, j=(e_, o_)[h]):
                ins = None
                for c in range(DC):
                    ins = eng.scalar_tensor_tensor(
                        tw0[:, :], bufx[h][:, c * TILE:(c + 1) * TILE],
                        1.0, PB[bcb[h]][:, :], ALU.mult, ALU.mult,
                        accum_out=xs_store[:, c * NT + j:c * NT + j + 1])
                return ins
            step("dve", fxsum, reads=xres[h] + (f"pb{bcb[h]}",),
                 writes=("tw0", f"xs_store{h}"))
        def fpk(eng, dsem, e_=e_, o_=o_):
            eng.dma_start(rmrp[:, e_ * 8:e_ * 8 + 4], P[0]).then_inc(dsem, 16)
            eng.dma_start(rmrp[:, e_ * 8 + 4:e_ * 8 + 8], M[0]).then_inc(dsem, 16)
            eng.dma_start(rmrp[:, o_ * 8:o_ * 8 + 4], P[1]).then_inc(dsem, 16)
            return eng.dma_start(rmrp[:, o_ * 8 + 4:o_ * 8 + 8],
                                 M[1]).then_inc(dsem, 16)
        step("dma", fpk, reads=(f"p0P{sfx}0", f"p0P{sfx}1", f"p0M{sfx}0",
                                f"p0M{sfx}1", "th4row"),
             writes=("rmrp",), n_dma=4, dcls="pack")
        drain_w(drains)

    def emit_gf1():
        def fmrs(eng):
            return eng.tensor_scalar(mrs[0:1, 0:8], mrs[0:1, 0:8], 1.0, 0.0, ALU.mult, ALU.add,
                                     accum_out=mrsR[0:1, :])
        step("dve", fmrs, reads=("mrs0",), writes=("mrs0", "mrsR"), strict=True)
        step("pe", lambda eng: eng.matmul(PB[3][:, 0:1], onesr[0:1, :], mrsR[0:1, :],
                                          start=True, stop=True),
             reads=("mrsR", "onesr"), writes=("pb3",))
        step("act", lambda eng: eng.activation(mrsb[:, :], PB[3][:, 0:1], AF.Identity),
             reads=("pb3",), writes=("mrsb",))
        def fxnm(eng):
            ins = None
            for c in range(DC):
                ins = eng.tensor_scalar(jr8[:, :], xs_store[:, c * NT:(c + 1) * NT], 1.0, 0.0,
                                        ALU.mult, ALU.add, accum_out=xnmf[:, c:c + 1])
            return ins
        step("dve", fxnm, reads=("xs_store0", "xs_store1"), writes=("jr8", "xnmf"), strict=True)
        def fxnm2(eng):
            return eng.tensor_scalar(xnmf[:, :], xnmf[:, :], mrsb[:, 0:1], 1.0 / 4096.0,
                                     ALU.subtract, ALU.mult)
        step("dve", fxnm2, reads=("xnmf", "mrsb"), writes=("xnmf",), strict=True)
        step("dve", lambda eng: eng.tensor_copy(xnmB[:, :], xnmf[:, :]),
             reads=("xnmf",), writes=("xnmB",), strict=True)
        def fm(eng):
            ins = None
            for k in range(8):
                for m in range(4):
                    ins = eng.matmul(PB[2 + m][:, 0:1],
                                     castb[:, k * 512 + m * 128:k * 512 + (m + 1) * 128],
                                     xnmB[:, k:k + 1], start=(k == 0), stop=(k == 7))
            return ins
        step("pe", fm, reads=tuple(f"castb{c}" for c in range(8)) + ("xnmB",),
             writes=("pb2", "pb3", "pb4", "pb5"))
        for m in range(4):
            def fe(eng, m=m):
                return eng.activation(hg1f[:, m:m + 1], PB[2 + m][:, 0:1], AF.Gelu,
                                      bias=bt["gb1"][:, m:m + 1])
            step("act", fe, reads=(f"pb{2 + m}", "b_gb1"), writes=("hg1f",))
        step("dve", lambda eng: eng.tensor_copy(hg1b[:, :], hg1f[:, :]),
             reads=("hg1f",), writes=("hg1b",))

    def emit_gf2():
        def fm(eng):
            ins = None
            for k in range(4):
                for m in range(2):
                    ins = eng.matmul(PB[6 + m][:, 0:1],
                                     h2b[:, k * 256 + m * 128:k * 256 + (m + 1) * 128],
                                     hg1b[:, k:k + 1], start=(k == 0), stop=(k == 3))
            return ins
        step("pe", fm, reads=("h2b", "hg1b"), writes=("pb6", "pb7"))
        for m in range(2):
            def fe(eng, m=m):
                return eng.activation(hg2f[:, m:m + 1], PB[6 + m][:, 0:1], AF.Gelu,
                                      bias=bt["gb2"][:, m:m + 1])
            step("act", fe, reads=(f"pb{6 + m}", "b_gb2"), writes=("hg2f",))
        step("dve", lambda eng: eng.tensor_copy(hg2b[:, :], hg2f[:, :]),
             reads=("hg2f",), writes=("hg2b",))

    def emit_gf3():
        def fm(eng):
            ins = None
            for k in range(2):
                ins = eng.matmul(PB[2][0:1, 0:1], g3B[:, k:k + 1], hg2b[:, k:k + 1],
                                 start=(k == 0), stop=(k == 1))
            return ins
        step("pe", fm, reads=("g3B", "hg2b"), writes=("pb2",))
        step("act", lambda eng: eng.activation(th3[0:1, :], PB[2][0:1, 0:1], AF.Tanh,
                                               scale=0.5, bias=g3h[0:1, 0:1]),
             reads=("pb2", "g3h"), writes=("th3",))
        def fc01a(eng):
            return eng.tensor_tensor(C0[0:1, :], bt["ar"][0:1, :], th3[0:1, :], ALU.mult)
        step("dve", fc01a, reads=("b_ar", "th3"), writes=("C0",))
        def fc01b(eng):
            eng.tensor_scalar(C0[0:1, :], C0[0:1, :], 0.7, 0.1, ALU.mult, ALU.add)
            return eng.tensor_scalar(C1[0:1, :], bt["ar"][0:1, :], 0.3, None, ALU.mult)
        step("dve", fc01b, reads=("b_ar", "C0"), writes=("C0", "C1"), strict=True)

    def unpack_rows(j):
        if j < 2:
            def f(eng, j=j):
                eng.tensor_copy(r_row[0:1, :], th4row[0:1, j * TILE:(j + 1) * TILE])
                return eng.tensor_copy(mr_row[0:1, :],
                                       p0rows[0:1, (2 + j) * TILE:(3 + j) * TILE])
            step("dve", f, reads=(f"p0P0{j}", f"p0M0{j}", "th4row"),
                 writes=("r_row", "mr_row"), strict=True)
            return
        def f(eng, dsem, j=j):
            eng.dma_start(r_row[0:1, :], rmrp[:, j * 8:j * 8 + 4]).then_inc(dsem, 16)
            return eng.dma_start(mr_row[0:1, :], rmrp[:, j * 8 + 4:j * 8 + 8]).then_inc(dsem, 16)
        step("dma", f, reads=("rmrp",), writes=("r_row", "mr_row"), n_dma=2, dcls="pack")

    def bcast_rows(j):
        b1 = pbank()
        b2 = pbank()
        def f1(eng, b1=b1):
            return eng.matmul(PB[b1][:, :], onesr[0:1, :], r_row[0:1, :],
                              start=True, stop=True)
        step("pe", f1, reads=("r_row", "onesr"), writes=(f"pb{b1}",))
        def f2(eng, b2=b2):
            return eng.matmul(PB[b2][:, :], onesr[0:1, :], mr_row[0:1, :],
                              start=True, stop=True)
        step("pe", f2, reads=("mr_row", "onesr"), writes=(f"pb{b2}",))
        def f3(eng, b1=b1):
            return eng.activation(rb_sb[:, :], PB[b1][:, :], AF.Identity)
        step("act", f3, reads=(f"pb{b1}",), writes=("rb_sb",))
        def f4(eng, b2=b2):
            return eng.activation(mrb_sb[:, :], PB[b2][:, :], AF.Identity)
        step("act", f4, reads=(f"pb{b2}",), writes=("mrb_sb",))

    def emit_xn(j):
        pi = j % 2
        for c in range(DC):
            def f(eng, pi=pi, c=c):
                eng.tensor_tensor(thT[:, 0:TILE], xbuf[pi][:, c * TILE:(c + 1) * TILE],
                                  rb_sb[:, :], ALU.mult)
                return eng.scalar_tensor_tensor(state[pi][:, c * TILE:(c + 1) * TILE],
                                                thT[:, 0:TILE], 1.0, mrb_sb[:, :],
                                                ALU.mult, ALU.subtract)
            step("dve", f, reads=xr(pi) + ("rb_sb", "mrb_sb"),
                 writes=("thT", stchunk(pi, c)))

    def emit_xnb(j, eng_name="pool"):
        pi = j % 2
        for h in range(2):
            if eng_name == "act":
                def f(eng, pi=pi, h=h):
                    return eng.activation(xnb[pi][:, h * 2048:(h + 1) * 2048],
                                          state[pi][:, h * 2048:(h + 1) * 2048],
                                          AF.Identity)
            else:
                def f(eng, pi=pi, h=h):
                    return eng.tensor_copy(xnb[pi][:, h * 2048:(h + 1) * 2048],
                                           state[pi][:, h * 2048:(h + 1) * 2048])
            step(eng_name, f, reads=tuple(stchunk(pi, c) for c in range(h * 4, h * 4 + 4)),
                 writes=(f"xnb{pi}" + ("lo" if h == 0 else "hi",)[0],))

    HB = [2, 3, 0]
    hrot = [0]

    def hbank():
        hrot[0] = (hrot[0] + 1) % 3
        return HB[hrot[0]]

    PBK = [1, 5, 6, 7]
    prot = [0]

    def pbank():
        prot[0] = (prot[0] + 1) % 4
        return PBK[prot[0]]

    castms = {}
    h1ev = {}
    h2ev = {}

    def emit_h1(j, k, ms):
        pi = j % 2
        if k == 1:
            rhs, rres = xnb[pi], (f"xnb{pi}lo", f"xnb{pi}hi")
            korder = tuple(range(8))
        else:
            rhs, rres = castb, tuple(f"castb{c}" for c in range(8))
            korder = (0, 4, 1, 5, 2, 6, 3, 7)
        for m in ms:
            b = hbank()
            if k > 1 and m == 0:
                cw = dict(castms)
                def f(eng, m=m, b=b, rhs=rhs, cw=cw, korder=korder):
                    ins = None
                    for i, kk in enumerate(korder):
                        we, wv = cw[kk]
                        eng.wait_ge(sem[we], wv)
                        ins = eng.matmul(PB[b][:, :],
                                         W1B[:, kk * 1024 + m * 128:kk * 1024 + (m + 1) * 128],
                                         rhs[:, kk * TILE:(kk + 1) * TILE],
                                         start=(i == 0), stop=(i == 7))
                    return ins
                step("pe", f, reads=("Ww1",), lazy_reads=rres, writes=(f"pb{b}",))
            else:
                def f(eng, m=m, b=b, rhs=rhs, korder=korder):
                    ins = None
                    for i, kk in enumerate(korder):
                        ins = eng.matmul(PB[b][:, :],
                                         W1B[:, kk * 1024 + m * 128:kk * 1024 + (m + 1) * 128],
                                         rhs[:, kk * TILE:(kk + 1) * TILE],
                                         start=(i == 0), stop=(i == 7))
                    return ins
                step("pe", f, reads=rres + ("Ww1",), writes=(f"pb{b}",))
            def fe(eng, m=m, b=b):
                return eng.activation(h1b[:, m * TILE:(m + 1) * TILE], PB[b][:, :], AF.Gelu,
                                      bias=bt["b1c"][:, m:m + 1])
            step("act", fe, reads=(f"pb{b}", "b_b1c"), writes=("h1b",))
            h1ev[m] = count["act"]

    def emit_h23z(j, k, defer_cb=False):
        t4off = (j % 2) * TILE if defer_cb else 0
        for m in range(4):
            b = hbank()
            if m == 0:
                ev = dict(h1ev)
                def f(eng, m=m, b=b, ev=ev):
                    ins = None
                    for kk in range(8):
                        eng.wait_ge(sem["act"], ev[kk])
                        ins = eng.matmul(PB[b][:, :],
                                         W2B[:, kk * 512 + m * 128:kk * 512 + (m + 1) * 128],
                                         h1b[:, kk * TILE:(kk + 1) * TILE],
                                         start=(kk == 0), stop=(kk == 7))
                    return ins
                step("pe", f, reads=("Ww2",), lazy_reads=("h1b",), writes=(f"pb{b}",))
            else:
                def f(eng, m=m, b=b):
                    ins = None
                    for kk in range(8):
                        ins = eng.matmul(PB[b][:, :],
                                         W2B[:, kk * 512 + m * 128:kk * 512 + (m + 1) * 128],
                                         h1b[:, kk * TILE:(kk + 1) * TILE],
                                         start=(kk == 0), stop=(kk == 7))
                    return ins
                step("pe", f, reads=("h1b", "Ww2"), writes=(f"pb{b}",))
            def fe(eng, m=m, b=b):
                return eng.activation(h2b[:, m * TILE:(m + 1) * TILE], PB[b][:, :], AF.Gelu,
                                      bias=bt["b2c"][:, m:m + 1])
            step("act", fe, reads=(f"pb{b}", "b_b2c"), writes=("h2b",))
            h2ev[m] = count["act"]
        for m in range(2):
            b = hbank()
            if m == 0:
                ev = dict(h2ev)
                def f(eng, m=m, b=b, ev=ev):
                    ins = None
                    for kk in range(4):
                        eng.wait_ge(sem["act"], ev[kk])
                        ins = eng.matmul(PB[b][:, :],
                                         W3B[:, kk * 256 + m * 128:kk * 256 + (m + 1) * 128],
                                         h2b[:, kk * TILE:(kk + 1) * TILE],
                                         start=(kk == 0), stop=(kk == 3))
                    return ins
                step("pe", f, reads=("Ww3",), lazy_reads=("h2b",), writes=(f"pb{b}",))
            else:
                def f(eng, m=m, b=b):
                    ins = None
                    for kk in range(4):
                        ins = eng.matmul(PB[b][:, :],
                                         W3B[:, kk * 256 + m * 128:kk * 256 + (m + 1) * 128],
                                         h2b[:, kk * TILE:(kk + 1) * TILE],
                                         start=(kk == 0), stop=(kk == 3))
                    return ins
                step("pe", f, reads=("h2b", "Ww3"), writes=(f"pb{b}",))
            def fe(eng, m=m, b=b):
                return eng.activation(h1b[:, m * TILE:(m + 1) * TILE], PB[b][:, :], AF.Gelu,
                                      bias=bt["b3c"][:, m:m + 1])
            step("act", fe, reads=(f"pb{b}", "b_b3c"), writes=("h1b",))
            h2ev["h3%d" % m] = count["act"]
        b = hbank()
        ev0 = h2ev["h30"]
        ev1 = h2ev["h31"]
        def fz4(eng, b=b, ev0=ev0, ev1=ev1):
            ins = None
            for kk in range(2):
                eng.wait_ge(sem["act"], ev0 if kk == 0 else ev1)
                ins = eng.matmul(PB[b][0:1, :], W4B[:, kk:kk + 1],
                                 h1b[:, kk * TILE:(kk + 1) * TILE],
                                 start=(kk == 0), stop=(kk == 1))
            return ins
        step("pe", fz4, reads=("Ww4",), lazy_reads=("h1b",), writes=(f"pb{b}",))
        def fth4(eng, b=b, t4off=t4off):
            return eng.activation(th4row[0:1, t4off:t4off + TILE], PB[b][0:1, :], AF.Tanh,
                                  scale=0.5, bias=b4h[0:1, 0:1])
        step("act", fth4, reads=(f"pb{b}", "b4h"), writes=("th4row",))
        if not defer_cb:
            emit_fcb(0)

    def emit_fcb(t4off):
        def fcb(eng, t4off=t4off):
            return eng.activation(cbrow[0:1, :], th4row[0:1, t4off:t4off + TILE],
                                  AF.Identity, scale=C1[0:1, 0:1], bias=C0[0:1, 0:1])
        step("act", fcb, reads=("th4row", "C1", "C0"), writes=("cbrow",), strict=True)

    def emit_cb(j, k):
        def fbc(eng):
            return eng.matmul(PB[4][:, :], onesr[0:1, :], cbrow[0:1, :], start=True, stop=True)
        step("pe", fbc, reads=("cbrow", "onesr"), writes=("pb4",))
        return 4

    def emit_upd_pair(j, cb, c):
        pi = j % 2
        sA = ALU.add if c < 2 else ALU.subtract
        sB = ALU.subtract if c < 2 else ALU.add
        def f(eng, pi=pi, c=c, sA=sA, sB=sB, cb=cb):
            cA = state[pi][:, c * TILE:(c + 1) * TILE]
            cB = state[pi][:, (c + 4) * TILE:(c + 5) * TILE]
            eng.tensor_tensor(tw0[:, :], PB[cb][:, :], cB, ALU.mult)
            eng.tensor_tensor(tw1[:, :], PB[cb][:, :], cA, ALU.mult)
            eng.tensor_tensor(cA, cA, tw0[:, :], sA)
            return eng.tensor_tensor(cB, cB, tw1[:, :], sB)
        step("dve", f, reads=(f"pb{cb}", stchunk(pi, c), stchunk(pi, c + 4)),
             writes=(stchunk(pi, c), stchunk(pi, c + 4), "tw0", "tw1"))

    def emit_upd_casts(j, cb, weave=None):
        weave = list(weave or [])
        pi = j % 2
        def fp0a(eng, pi=pi, cb=cb):
            cA = state[pi][:, 0:TILE]
            cB = state[pi][:, 4 * TILE:5 * TILE]
            eng.tensor_tensor(tw0[:, :], PB[cb][:, :], cB, ALU.mult)
            eng.tensor_tensor(tw1[:, :], PB[cb][:, :], cA, ALU.mult)
            eng.tensor_tensor(cA, cA, tw0[:, :], ALU.add)
            return eng.tensor_copy(castb[:, 0:TILE], cA)
        step("dve", fp0a, reads=(f"pb{cb}", stchunk(pi, 0), stchunk(pi, 4)),
             writes=(stchunk(pi, 0), "tw0", "tw1", "castb0"))
        castms[0] = ("dve", count["dve"])
        def fp0b(eng, pi=pi):
            cB = state[pi][:, 4 * TILE:5 * TILE]
            eng.tensor_tensor(cB, cB, tw1[:, :], ALU.subtract)
            return eng.tensor_copy(castb[:, 4 * TILE:5 * TILE], cB)
        step("dve", fp0b, reads=("tw1", stchunk(pi, 4)),
             writes=(stchunk(pi, 4), "castb4"))
        castms[4] = ("dve", count["dve"])
        for c in range(1, 4):
            emit_upd_pair(j, cb, c)
            for cc in (c, c + 4):
                def fcst(eng, pi=pi, cc=cc):
                    return eng.activation(castb[:, cc * TILE:(cc + 1) * TILE],
                                          state[pi][:, cc * TILE:(cc + 1) * TILE],
                                          AF.Identity)
                step("act", fcst, reads=(stchunk(pi, cc),), writes=(f"castb{cc}",))
                castms[cc] = ("act", count["act"])
                if weave:
                    emit_proj_evac(*weave.pop(0))
        while weave:
            emit_proj_evac(*weave.pop(0))

    def emit_proj_mm(j, which):
        pi = j % 2
        wmap = {"a": (WaB, "WWa"), "b": (WbB, "WWb"), "c": (WcB, "WWc")}
        WB, wres = wmap[which]
        out = []
        for n in range(NC):
            b = pbank()
            def f(eng, n=n, b=b, pi=pi, WB=WB):
                ins = None
                for kk in range(8):
                    ins = eng.matmul(PB[b][:, :],
                                     WB[:, kk * 256 + n * 128:kk * 256 + (n + 1) * 128],
                                     xnb[pi][:, kk * TILE:(kk + 1) * TILE],
                                     start=(kk == 0), stop=(kk == 7))
                return ins
            step("pe", f, reads=(f"xnb{pi}lo", f"xnb{pi}hi", wres), writes=(f"pb{b}",))
            out.append((which, n, b))
        return out

    def emit_proj_evac(which, n, b):
        if which == "a":
            def fe(eng, n=n, b=b):
                return eng.activation(thT[:, n * TILE:(n + 1) * TILE], PB[b][:, :], AF.Tanh,
                                      scale=0.5, bias=ba_h[:, n:n + 1])
            step("act", fe, reads=(f"pb{b}", "ba_h"), writes=("thT",))
        elif which == "b":
            def fe(eng, n=n, b=b):
                return eng.activation(buT[:, n * TILE:(n + 1) * TILE], PB[b][:, :],
                                      AF.Identity, bias=bt["bb2"][:, n:n + 1])
            step("act", fe, reads=(f"pb{b}", "b_bb2"), writes=("buT",))
        else:
            def fe(eng, n=n, b=b):
                return eng.activation(cT[:, n * TILE:(n + 1) * TILE], PB[b][:, :], AF.Tanh,
                                      bias=bt["bc2"][:, n:n + 1])
            step("act", fe, reads=(f"pb{b}", "b_bc2"), writes=("cT",))

    def emit_proj(j, which):
        for which_, n, b in emit_proj_mm(j, which):
            emit_proj_evac(which_, n, b)

    def emit_scan(j):
        def f1(eng):
            eng.tensor_scalar(aT[:, :], thT[:, :], 0.5, 0.5, ALU.mult, ALU.add)
            return eng.tensor_scalar(thT[:, :], thT[:, :], -0.5, 0.5, ALU.mult, ALU.add)
        step("dve", f1, reads=("thT",), writes=("aT", "thT"))
        def f2(eng):
            return eng.tensor_tensor(buT[:, :], thT[:, :], buT[:, :], ALU.mult)
        step("dve", f2, reads=("thT", "buT"), writes=("buT",))
        for n in range(NC):
            def f3(eng, n=n):
                return eng.tensor_tensor_scan(buT[:, n * TILE:(n + 1) * TILE],
                                              aT[:, n * TILE:(n + 1) * TILE],
                                              buT[:, n * TILE:(n + 1) * TILE],
                                              carry[:, n:n + 1], ALU.mult, ALU.add)
            step("dve", f3, reads=("aT", "buT", "carry"), writes=("buT",))
        def f4(eng):
            ins = None
            for n in range(NC):
                ins = eng.tensor_copy(carry[:, n:n + 1],
                                      buT[:, n * TILE + TILE - 1:n * TILE + TILE])
            return ins
        step("dve", f4, reads=("buT",), writes=("carry",), strict=True)

    def emit_cs(j):
        def f(eng):
            return eng.tensor_tensor(csT[:, :], cT[:, :], buT[:, :], ALU.mult)
        step("dve", f, reads=("cT", "buT"), writes=("csT",))

    ybank = {}

    def emit_ygroup(j, m):
        pi = j % 2
        b = pbank()
        ybank[(j, m)] = b
        def fy(eng, m=m, b=b, pi=pi):
            ins = None
            for kk in range(8):
                ins = eng.matmul(PB[b][:, :],
                                 WdB[:, kk * 1024 + m * 128:kk * 1024 + (m + 1) * 128],
                                 xnb[pi][:, kk * TILE:(kk + 1) * TILE],
                                 start=(kk == 0), stop=False)
            for kk in range(2):
                ins = eng.matmul(PB[b][:, :],
                                 WyB[:, kk * 1024 + m * 128:kk * 1024 + (m + 1) * 128],
                                 csT[:, kk * TILE:(kk + 1) * TILE],
                                 start=False, stop=(kk == 1))
            return ins
        step("pe", fy, reads=(f"xnb{pi}lo", f"xnb{pi}hi", "csT", "WWd", "WWy"),
             writes=(f"pb{b}",))

    def emit_pfold(j, m):
        pi = j % 2
        b = ybank[(j, m)]
        def fp(eng, m=m, b=b):
            return eng.scalar_tensor_tensor(gbuf[:, m * TILE:(m + 1) * TILE], PB[b][:, :],
                                            byd[:, m:m + 1],
                                            gbuf[:, m * TILE:(m + 1) * TILE],
                                            ALU.add, ALU.mult)
        step("dve", fp, reads=(f"pb{b}", "byd", f"g{m}"), writes=(f"g{m}",))
        def fo(eng, c=m, pi=pi):
            return eng.tensor_tensor(xbuf[pi][:, c * TILE:(c + 1) * TILE],
                                     gbuf[:, c * TILE:(c + 1) * TILE],
                                     xbuf[pi][:, c * TILE:(c + 1) * TILE], ALU.add)
        step("dve", fo, reads=(f"g{m}", xr(pi)[m // 4]), writes=(xr(pi)[m // 4],))

    def emit_silu_chunk(j, cc):
        pi = j % 2
        def fs(eng, pi=pi, cc=cc):
            return eng.activation(gbuf[:, cc * TILE:(cc + 1) * TILE],
                                  state[pi][:, cc * TILE:(cc + 1) * TILE], AF.Silu)
        step("act", fs, reads=(stchunk(pi, cc),), writes=(f"g{cc}",))

    def emit_outdma(j, h):
        pi = j % 2
        def fo(eng, dsem, pi=pi, j=j, h=h):
            dst = bass.AP(oT, j * TILE + h * 4 * 128 * T,
                          [[T, 128], [128 * T, DC // 2], [1, TILE]])
            return eng.dma_start(dst, xbuf[pi][:, h * 2048:(h + 1) * 2048],
                                 ).then_inc(dsem, 16)
        step("dma", fo, reads=(xr(pi)[h],), n_dma=1, dcls="out")

    load_x(0)
    load_x(1)
    emit_p0_stage(0)
    step("dma", fbias, writes=tuple("b_" + n for n in db), n_dma=len(db), dcls="bias")
    drain_w(2)
    emit_p0_pair(0, drains=2)
    step("dve", bias_prep,
         reads=("b_ba2", "b_b4", "b_gb3", "b_bd8", "b_by8"),
         writes=("ba_h", "b4h", "g3h", "byd"))
    unpack_rows(0)
    bcast_rows(0)
    emit_xn(0)
    if debug:
        dbg("xn0", state[0][:, :], [stchunk(0, c) for c in range(8)])
    load_x(2)
    unpack_rows(1)
    bcast_rows(1)
    emit_xn(1)
    load_x(3)
    emit_p0_stage(1)
    load_x(4)
    load_x(5)
    emit_p0_pair(1, drains=2)
    emit_p0_stage(2)
    load_x(6)
    load_x(7)
    emit_p0_pair(2, drains=2)
    emit_p0_stage(3)
    emit_p0_pair(3, drains=2)
    if debug:
        dbg("rmrp", rmrp[:, :], ["rmrp"])
        dbg("mrs", mrs[0:1, :], ["mrs0"])
    emit_xnb(0, "act")
    drain_w(4)
    need_w("w1")
    emit_h1(0, 1, range(8))
    allow_g[0] = True
    need_w("g1")
    need_w("w2")
    need_w("w3")
    need_w("w4")
    emit_h23z(0, 1, defer_cb=True)
    emit_xnb(1, "act")
    need_w("g2")
    need_w("g3")
    if debug:
        dbg("th4row0", th4row[0:1, 0:TILE], ["th4row"])
    need_w("Wa")
    emit_proj(0, "a")
    need_w("Wb")
    if debug:
        dbg("WbEarly", WbB[:, :], ["WWb"])
    emit_proj(0, "b")
    need_w("Wc")
    emit_proj(0, "c")
    if debug:
        dbg("thT0", thT[:, :], ["thT"])
        dbg("buT0", buT[:, :], ["buT"])
    drain_w(10 ** 9)
    drain_cv(10 ** 9, force=True)
    load_x(0)
    load_x(1)
    if debug:
        dbg("WaBd", WaB[:, :], ["WWa"])
        dbg("WbBd", WbB[:, :], ["WWb"])
        dbg("WcBd", WcB[:, :], ["WWc"])
        dbg("W4Bd", W4B[:, :], ["Ww4"])
        dbg("W1Bd", W1B[:, :], ["Ww1"])
        dbg("g1Bd", castb[:, :], list(CASTR))
        dbg("g2Bd", h2b[:, 0:1024], ["h2b"])
        dbg("g3Bd", g3B[:, :], ["g3B"])

    emit_gf1()
    if debug:
        dbg("hg1f", hg1f[:, :], ["hg1f"])
    emit_gf2()
    if debug:
        dbg("hg2f", hg2f[:, :], ["hg2f"])
    emit_gf3()
    if debug:
        dbg("xnmf", xnmf[:, :], ["xnmf"])
        dbg("C0", C0[0:1, :], ["C0"])
        dbg("C1", C1[0:1, :], ["C1"])
        dbg("th3", th3[0:1, :], ["th3"])

    def emit_tile_ffn_tail(j, weave_next, lead_full=True):
        emit_h1(j, 2, range(8))
        emit_h23z(j, 2)
        if weave_next and lead_full:
            bcast_rows(j + 1)
        emit_ygroup(j, 4)
        cb = emit_cb(j, 2)
        emit_upd_casts(j, cb)
        if weave_next and lead_full:
            emit_xn(j + 1)
        emit_h1(j, 3, range(8))
        emit_h23z(j, 3)
        emit_ygroup(j, 0)
        if j == 0 and debug:
            dbg("state0c2", state[0][:, :], [stchunk(0, c) for c in range(8)])
        cb = emit_cb(j, 3)
        emit_upd_pair(j, cb, 0)
        emit_silu_chunk(j, 0)
        emit_silu_chunk(j, 4)
        emit_ygroup(j, 1)
        emit_pfold(j, 0)
        emit_pfold(j, 4)
        if weave_next:
            if lead_full:
                emit_xnb(j + 1, "act")
            emit_h1(j + 1, 1, range(0, 4))
        emit_upd_pair(j, cb, 1)
        emit_silu_chunk(j, 1)
        emit_silu_chunk(j, 5)
        emit_ygroup(j, 5)
        emit_pfold(j, 1)
        emit_pfold(j, 5)
        if weave_next:
            emit_h1(j + 1, 1, range(4, 8))
        emit_upd_pair(j, cb, 2)
        emit_silu_chunk(j, 2)
        emit_silu_chunk(j, 6)
        emit_ygroup(j, 2)
        emit_pfold(j, 2)
        emit_upd_pair(j, cb, 3)
        emit_silu_chunk(j, 3)
        emit_silu_chunk(j, 7)
        emit_ygroup(j, 6)
        emit_pfold(j, 6)
        emit_ygroup(j, 3)
        emit_pfold(j, 3)
        emit_outdma(j, 0)
        emit_ygroup(j, 7)
        emit_pfold(j, 7)
        if j == 0 and debug:
            dbg("p0", gbuf[:, :], [f"g{c}" for c in range(8)])
            dbg("out0", xbuf[0][:, :], ["xbuf0lo", "xbuf0hi"])
        emit_outdma(j, 1)

    emit_scan(0)
    emit_cs(0)
    if debug:
        dbg("aT0", aT[:, :], ["aT"])
        dbg("s0", buT[:, :], ["buT"])
        dbg("cT0", cT[:, :], ["cT"])
        dbg("cs0", csT[:, :], ["csT"])

    emit_fcb(0)
    cb = emit_cb(0, 1)
    emit_upd_casts(0, cb)
    if debug:
        dbg("state0c1", state[0][:, :], [stchunk(0, c) for c in range(8)])
    emit_tile_ffn_tail(0, weave_next=True, lead_full=False)
    load_x(2)
    unpack_rows(2)

    for j in range(1, NT):
        emit_h23z(j, 1)
        emit_proj(j, "a")
        cb = emit_cb(j, 1)
        emit_proj(j, "b")
        emit_proj(j, "c")
        emit_upd_casts(j, cb)
        emit_scan(j)
        emit_cs(j)
        emit_tile_ffn_tail(j, weave_next=(j + 1 < NT))
        if j + 2 < NT:
            load_x(j + 2)
            unpack_rows(j + 2)

    per_eng = {k: [] for k in ["pe", "act", "dve", "pool", "dma"]}
    for s in steps:
        per_eng[s["eng"]].append(s)

    def emit_for(eng_name):
        def run(engobj):
            maxw = {}
            for s in per_eng[eng_name]:
                for d in sorted(s["deps"]):
                    ps = steps[d]
                    if (ps["eng"] == eng_name and eng_name != "dma"
                            and not ps["is_dma"] and not s["strict"]):
                        continue
                    k = ps["key"]
                    val = ps["val"]
                    if maxw.get(k, -1) >= val:
                        continue
                    engobj.wait_ge(sem[k], val)
                    maxw[k] = val
                if s["is_dma"]:
                    s["fn"](engobj, sem[s["key"]])
                else:
                    last = s["fn"](engobj)
                    last.then_inc(sem[s["eng"]], 1)
            if eng_name == "dma":
                for k in DCLS:
                    if count[k]:
                        engobj.wait_ge(sem[k], count[k])
        return run

    with nc.Block() as block:
        @block.tensor
        def _(tensor):
            emit_for("pe")(tensor)

        @block.scalar
        def _(scalar):
            emit_for("act")(scalar)

        @block.vector
        def _(vector):
            emit_for("dve")(vector)

        @block.gpsimd
        def _(gpsimd):
            emit_for("pool")(gpsimd)

        @block.sync
        def _(sync):
            emit_for("dma")(sync)

    ctx.close()
    nc._dbg = dbg_tensors
    return nc


def _cvec(v, ncols):
    return np.ascontiguousarray(np.asarray(v, np.float32).reshape(ncols, 128).T)


def kernel(**inputs):
    g = lambda k: np.asarray(inputs[k], np.float32)
    x = g("L0_attn_x")
    base = {
        "Wa": g("L0_attn_Wa"), "Wb": g("L0_attn_Wb"), "Wc": g("L0_attn_Wc"),
        "Wd": g("L0_attn_Wd"), "Wy": g("L0_attn_Wy"),
        "w1": g("L0_ffn_cn_w1"), "w2": g("L0_ffn_cn_w2"), "w3": g("L0_ffn_cn_w3"),
        "w4": g("L0_ffn_cn_w4"),
        "g1": g("L0_ffn_gc_w1"), "g2": g("L0_ffn_gc_w2"), "g3": g("L0_ffn_gc_w3"),
        "ba2": _cvec(g("L0_attn_ba"), 2), "bb2": _cvec(g("L0_attn_bb"), 2),
        "bc2": _cvec(g("L0_attn_bc"), 2),
        "bd8": _cvec(g("L0_attn_bd"), 8), "by8": _cvec(g("L0_attn_by"), 8),
        "b1c": _cvec(g("L0_ffn_cn_b1"), 8), "b2c": _cvec(g("L0_ffn_cn_b2"), 4),
        "b3c": _cvec(g("L0_ffn_cn_b3"), 2),
        "b4": g("L0_ffn_cn_b4").reshape(1, 1),
        "gb1": _cvec(g("L0_ffn_gc_b1"), 4), "gb2": _cvec(g("L0_ffn_gc_b2"), 2),
        "gb3": g("L0_ffn_gc_b3").reshape(1, 1),
        "ar": g("L0_ffn_adaptive_range").reshape(1, 1),
    }
    base = {k: np.ascontiguousarray(v) for k, v in base.items()}
    in_maps = []
    for b in range(8):
        m = dict(base)
        m["xT"] = np.ascontiguousarray(x[b].T)
        in_maps.append(m)
    if "nc" not in _NC_CACHE:
        _NC_CACHE["nc"] = build_nc()
    res = run_bass_kernel_spmd(_NC_CACHE["nc"], in_maps, core_ids=list(range(8)))
    out = np.stack([np.asarray(res.results[b]["oT"]).T for b in range(8)])
    return np.ascontiguousarray(out.astype(np.float32))


if __name__ == "__main__":
    nc = build_nc()
    print("built ok")


# version 86
# speedup vs baseline: 1.0475x; 1.0475x over previous
import numpy as np
import concourse.bass as bass
import concourse.mybir as mybir
from concourse.bass_utils import run_bass_kernel_spmd
from contextlib import ExitStack

F32 = mybir.dt.float32
BF16 = mybir.dt.bfloat16
AF = mybir.ActivationFunctionType
ALU = mybir.AluOpType

D = 1024
T = 4096
NS = 256
TILE = 512
NT = T // TILE
DC = D // 128
NC = NS // 128
EPS = 1e-5

_NC_CACHE = {}


def build_nc(debug=False):
    nc = bass.Bass("TRN2", target_bir_lowering=False, dynamic_dma_scratch_size=1024)

    xT = nc.dram_tensor("xT", [D, T], F32, kind="ExternalInput")
    oT = nc.dram_tensor("oT", [D, T], F32, kind="ExternalOutput")
    dw = {}
    for name, shp in [("Wa", [D, NS]), ("Wb", [D, NS]), ("Wc", [D, NS]),
                      ("Wd", [D, D]), ("Wy", [NS, D]),
                      ("w1", [D, D]), ("w2", [D, 512]), ("w3", [512, 256]), ("w4", [256, 1]),
                      ("g1", [D, 512]), ("g2", [512, 256]), ("g3", [256, 1])]:
        dw[name] = nc.dram_tensor(name, shp, F32, kind="ExternalInput")
    bspec = [("ba2", [128, 2]), ("bb2", [128, 2]), ("bc2", [128, 2]),
             ("bd8", [128, 8]), ("by8", [128, 8]),
             ("b1c", [128, 8]), ("b2c", [128, 4]), ("b3c", [128, 2]), ("b4", [1, 1]),
             ("gb1", [128, 4]), ("gb2", [128, 2]), ("gb3", [1, 1]), ("ar", [1, 1])]
    db = {name: nc.dram_tensor(name, shp, F32, kind="ExternalInput") for name, shp in bspec}

    ctx = ExitStack()
    e = ctx.enter_context

    WaB = e(nc.sbuf_tensor("WaB", [128, 8 * 256], BF16))
    WbB = e(nc.sbuf_tensor("WbB", [128, 8 * 256], BF16))
    WcB = e(nc.sbuf_tensor("WcB", [128, 8 * 256], BF16))
    WdB = e(nc.sbuf_tensor("WdB", [128, 8 * 1024], BF16))
    WyB = e(nc.sbuf_tensor("WyB", [128, 2 * 1024], BF16))
    W1B = e(nc.sbuf_tensor("W1B", [128, 8 * 1024], BF16))
    W2B = e(nc.sbuf_tensor("W2B", [128, 8 * 512], BF16))
    W3B = e(nc.sbuf_tensor("W3B", [128, 4 * 256], BF16))
    W4B = e(nc.sbuf_tensor("W4B", [128, 2 * 1], BF16))
    bt = {name: e(nc.sbuf_tensor("t_" + name, shp, F32)) for name, shp in bspec}
    ba_h = e(nc.sbuf_tensor("ba_h", [128, 2], F32))
    b4h = e(nc.sbuf_tensor("b4h", [1, 1], F32))
    g3h = e(nc.sbuf_tensor("g3h", [1, 1], F32))
    byd = e(nc.sbuf_tensor("byd", [128, 8], F32))
    onesr = e(nc.sbuf_tensor("onesr", [65, 128], F32))
    onescb = e(nc.sbuf_tensor("onescb", [128, 1], BF16))
    rmrp = e(nc.sbuf_tensor("rmrp", [128, NT * 8], F32))
    xs_store = e(nc.sbuf_tensor("xs_store", [128, DC * NT], F32))
    mrs = e(nc.sbuf_tensor("mrs", [1, NT], F32))
    mrsR = e(nc.sbuf_tensor("mrsR", [1, 1], F32))
    mrsb = e(nc.sbuf_tensor("mrsb", [128, 1], F32))
    xnmf = e(nc.sbuf_tensor("xnmf", [128, DC], F32))
    jr8 = e(nc.sbuf_tensor("jr8", [128, 8], F32))
    hg1f = e(nc.sbuf_tensor("hg1f", [128, 4], F32))
    hg2f = e(nc.sbuf_tensor("hg2f", [128, 2], F32))
    th3 = e(nc.sbuf_tensor("th3", [1, 1], F32))
    C0 = e(nc.sbuf_tensor("C0", [1, 1], F32))
    C1 = e(nc.sbuf_tensor("C1", [1, 1], F32))
    th4row = e(nc.sbuf_tensor("th4row", [65, 2 * TILE], F32))
    cbrow = e(nc.sbuf_tensor("cbrow", [1, TILE], F32))
    r_row = e(nc.sbuf_tensor("r_row", [1, TILE], F32))
    mr_row = e(nc.sbuf_tensor("mr_row", [1, TILE], F32))
    carry = e(nc.sbuf_tensor("carry", [128, NC], F32))
    eps_t = e(nc.sbuf_tensor("eps_t", [1, 1], F32))
    p0rows = e(nc.sbuf_tensor("p0rows", [65, 4 * TILE], F32))

    xbuf = [e(nc.sbuf_tensor(f"xbuf{p}", [128, DC * TILE], F32)) for p in range(2)]
    xnb = [e(nc.sbuf_tensor(f"xnb{p}", [128, DC * TILE], BF16)) for p in range(2)]
    castb = e(nc.sbuf_tensor("castb", [128, DC * TILE], BF16))
    state = [e(nc.sbuf_tensor(f"state{p}", [128, DC * TILE], F32)) for p in range(2)]
    gbuf = e(nc.sbuf_tensor("gbuf", [128, DC * TILE], F32))
    h1b = e(nc.sbuf_tensor("h1b", [128, 8 * TILE], BF16))
    h2b = e(nc.sbuf_tensor("h2b", [128, 4 * TILE], BF16))
    tw0 = e(nc.sbuf_tensor("tw0", [128, TILE], F32))
    tw1 = e(nc.sbuf_tensor("tw1", [128, TILE], F32))
    thT = e(nc.sbuf_tensor("thT", [128, NC * TILE], F32))
    aT = e(nc.sbuf_tensor("aT", [128, NC * TILE], F32))
    buT = e(nc.sbuf_tensor("buT", [128, NC * TILE], F32))
    cT = e(nc.sbuf_tensor("cT", [128, NC * TILE], F32))
    csT = e(nc.sbuf_tensor("csT", [128, NC * TILE], BF16))
    g3B = e(nc.sbuf_tensor("g3B", [128, 2], BF16))
    xnmB = e(nc.sbuf_tensor("xnmB", [128, DC], BF16))
    hg1b = e(nc.sbuf_tensor("hg1b", [128, 4], BF16))
    hg2b = e(nc.sbuf_tensor("hg2b", [128, 2], BF16))
    rb_sb = e(nc.sbuf_tensor("rb_sb", [128, TILE], F32))
    mrb_sb = e(nc.sbuf_tensor("mrb_sb", [128, TILE], F32))

    rem = nc.sbuf_bytes_remaining
    assert rem >= 0, f"SBUF over budget by {-rem} bytes"
    import sys
    print(f"[build] sbuf_bytes_remaining={rem}", file=sys.stderr)

    PB = [e(nc.psum_tensor(f"PB{i}", [128, TILE], F32)) for i in range(8)]
    DCLS = ["bias", "wst0", "wst1", "wst2", "wst3", "x0", "x1", "x0h", "x1h",
            "pack", "out", "dbg"]
    sem = {k: e(nc.semaphore("sem_" + k)) for k in ["pe", "act", "dve", "pool"] + DCLS}

    dbg_tensors = {}

    steps = []
    count = {k: 0 for k in ["pe", "act", "dve", "pool"] + DCLS}
    last_w = {}
    readers = {}

    def step(eng, fn, reads=(), writes=(), n_dma=0, strict=False, dcls=None,
             lazy_reads=()):
        deps = set()
        for r in reads:
            if r in last_w:
                deps.add(last_w[r])
        for r in writes:
            deps |= readers.get(r, set())
            if r in last_w:
                deps.add(last_w[r])
        sid = len(steps)
        is_dma = n_dma > 0
        key = dcls if is_dma else eng
        if is_dma:
            count[key] += 16 * n_dma
        else:
            count[key] += 1
        steps.append(dict(eng=eng, fn=fn, deps=deps, val=count[key], strict=strict,
                          key=key, is_dma=is_dma))
        for r in reads:
            readers.setdefault(r, set()).add(sid)
        for r in lazy_reads:
            readers.setdefault(r, set()).add(sid)
        for r in writes:
            last_w[r] = sid
            readers[r] = {sid}
        return sid

    def stchunk(pi, c):
        return f"st{pi}c{c}"

    def dbg(name, ap, reads):
        if not debug:
            return
        shp = [ap.shape[0], int(np.prod(ap.shape[1:]))]
        dt_ = nc.dram_tensor("dbg_" + name, shp, ap.dtype, kind="ExternalOutput")
        dbg_tensors[name] = dt_
        def f(eng, dsem, dt_=dt_, ap=ap):
            return eng.dma_start(dt_[:, :], ap).then_inc(dsem, 16)
        step("dma", f, reads=tuple(reads), n_dma=1, dcls="dbg")

    def consts_prep(eng):
        eng.memset(onesr[:, :], 1.0)
        eng.memset(eps_t[:, :], EPS)
        eng.memset(carry[:, :], 0.0)
        return eng.memset(onescb[:, :], 1.0)
    step("dve", consts_prep, writes=("onesr", "eps_t", "onescb", "carry"))

    def fbias(eng, dsem):
        ins = None
        for name in db:
            ins = eng.dma_start(bt[name][:, :], db[name][:, :]).then_inc(dsem, 16)
        return ins

    def bias_prep(eng):
        eng.tensor_scalar(ba_h[:, :], bt["ba2"][:, :], 0.5, None, ALU.mult)
        eng.tensor_scalar(b4h[:, :], bt["b4"][:, :], 0.5, None, ALU.mult)
        eng.tensor_scalar(g3h[:, :], bt["gb3"][:, :], 0.5, None, ALU.mult)
        return eng.tensor_tensor(byd[:, :], bt["bd8"][:, :], bt["by8"][:, :], ALU.add)

    wspec = [("w1", W1B, 8, 1024), ("w2", W2B, 8, 512), ("w3", W3B, 4, 256), ("w4", W4B, 2, 1),
             ("Wa", WaB, 8, 256), ("Wb", WbB, 8, 256), ("Wc", WcB, 8, 256),
             ("Wd", WdB, 8, 1024), ("Wy", WyB, 2, 1024)]
    bigbufs = [(gbuf, 0, tuple(f"wstg0c{c}" for c in range(4)),
                tuple(f"g{c}" for c in range(4))),
               (gbuf, 2048, tuple(f"wstg1c{c}" for c in range(4)),
                tuple(f"g{c}" for c in range(4, 8)))]
    wpieces = []
    for name, wb, nk, M in wspec:
        nh = max(1, M // TILE)
        for k in range(nk):
            for h in range(nh):
                wpieces.append((name, wb, M, k, h * TILE, min(M, h * TILE + TILE),
                                ("W" + name,)))
    for k in range(8):
        wpieces.append(("g1", castb, 512, k, 0, 512, (f"castb{k}",)))
    for k in range(4):
        wpieces.append(("g2", h2b, 256, k, 0, 256, ("h2b",)))
    for k in range(2):
        wpieces.append(("g3", g3B, 1, k, 0, 1, ("g3B",)))
    bigjobs = []
    bi = 0
    while wpieces:
        grp = [wpieces.pop(0)]
        while wpieces and len(grp) < 4 and wpieces[0][0] == grp[0][0]:
            grp.append(wpieces.pop(0))
        bigjobs.append((grp, bi % 2))
        bi += 1

    pend_cv = []
    wround = [0]

    def emit_cvs(cvs):
        for eng_name, fcp, bres, wres in cvs:
            step(eng_name, fcp, reads=bres, writes=wres)

    def drain_cv(njobs, force=False):
        while pend_cv and njobs > 0:
            if not force and pend_cv[0][2] > wround[0] - 2:
                break
            emit_cvs(pend_cv.pop(0)[3])
            njobs -= 1

    allow_g = [False]

    def drain_w(n):
        if n <= 0:
            return
        wround[0] += 1
        drain_cv(n)
        while bigjobs and n > 0:
            if not allow_g[0] and bigjobs[0][0][0][0] in ("g1", "g2", "g3"):
                break
            grp, p = bigjobs.pop(0)
            for _i, ent in enumerate(pend_cv):
                if ent[0] == p:
                    emit_cvs(ent[3])
                    pend_cv.pop(_i)
                    break
            buf, boff, bres, gres = bigbufs[p]
            name = grp[0][0]
            d = dw[name]
            M = grp[0][2]
            npc = len(grp)
            k0, w00 = grp[0][3], grp[0][4]
            pw = grp[0][5] - grp[0][4]
            if M <= TILE:
                def fl(eng, dsem, d=d, buf=buf, boff=boff, k0=k0, npc=npc, pw=pw, M=M):
                    srcap = bass.AP(d, k0 * 128 * M, [[M, 128], [128 * M, npc], [1, pw]])
                    dstap = bass.AP(buf, boff, [[DC * TILE, 128], [TILE, npc], [1, pw]])
                    return eng.dma_start(dstap, srcap,
                                         allow_slow_non_contiguous=(pw < 16)).then_inc(dsem, 16)
            else:
                assert npc == 4 and w00 == 0 and M == 2 * TILE
                def fl(eng, dsem, d=d, buf=buf, boff=boff, k0=k0, M=M):
                    srcap = bass.AP(d, k0 * 128 * M, [[M, 128], [128 * M, 2], [1, M]])
                    dstap = bass.AP(buf, boff, [[DC * TILE, 128], [M, 2], [1, M]])
                    return eng.dma_start(dstap, srcap).then_inc(dsem, 16)
            nd = 1
            step("dma", fl, writes=bres + gres, n_dma=nd, dcls=f"wst{p}")
            cvs = []
            for i, (nm, wb, M_, kk_, w0_, w1_, wres) in enumerate(grp):
                if nm in ("g1", "g2", "g3", "w1", "w2", "Wd", "Wy"):
                    eng_name = "act"
                else:
                    eng_name = "pool"
                if eng_name == "act":
                    def fcp(eng, wb=wb, M_=M_, kk_=kk_, w0_=w0_, w1_=w1_, buf=buf, boff=boff, i=i):
                        return eng.activation(wb[:, kk_ * M_ + w0_:kk_ * M_ + w1_],
                                              buf[:, boff + i * TILE:boff + i * TILE + (w1_ - w0_)],
                                              AF.Identity)
                else:
                    def fcp(eng, wb=wb, M_=M_, kk_=kk_, w0_=w0_, w1_=w1_, buf=buf, boff=boff, i=i):
                        return eng.tensor_copy(wb[:, kk_ * M_ + w0_:kk_ * M_ + w1_],
                                               buf[:, boff + i * TILE:boff + i * TILE + (w1_ - w0_)])
                cvs.append((eng_name, fcp, bres + gres, wres))
            pend_cv.append((p, name, wround[0], cvs))
            n -= 1

    def need_w(wname):
        while any(grp[0][0] == wname for grp, _p in bigjobs):
            drain_w(1)
        for ent in [ent for ent in list(pend_cv) if ent[1] == wname]:
            pend_cv.remove(ent)
            emit_cvs(ent[3])

    def load_x(j):
        pi = j % 2
        for h in range(2):
            def f(eng, dsem, j=j, h=h):
                src = bass.AP(xT, j * TILE + h * 4 * 128 * T,
                              [[T, 128], [128 * T, DC // 2], [1, TILE]])
                return eng.dma_start(xbuf[pi][:, h * 2048:(h + 1) * 2048],
                                     src).then_inc(dsem, 16)
            step("dma", f, writes=(f"xbuf{pi}" + ("lo", "hi")[h],),
                 n_dma=1, dcls=f"x{pi}" + ("", "h")[h])

    def xr(pi):
        return (f"xbuf{pi}lo", f"xbuf{pi}hi")

    CASTR = tuple(f"castb{c}" for c in range(8))

    def p0bufs(q):
        bx = {0: xnb[0], 1: xnb[1]}
        xres = {0: ("xnb0lo", "xnb0hi"), 1: ("xnb1lo", "xnb1hi")}
        bq = {0: castb, 1: h1b}
        qres = {0: CASTR, 1: ("h1b",)}
        if q % 2:
            return (bq, qres, bx, xres)
        return (bx, xres, bq, qres)

    def p0slots(q):
        a = (q % 2)
        pp = 64 * a
        P = {h: th4row[pp:pp + 1, h * TILE:(h + 1) * TILE] for h in range(2)}
        Q = {h: p0rows[pp:pp + 1, h * TILE:(h + 1) * TILE] for h in range(2)}
        M = {h: p0rows[pp:pp + 1, (2 + h) * TILE:(3 + h) * TILE] for h in range(2)}
        bank = {0: 4 - 2 * a, 1: 5 - 2 * a}
        bcb = {0: 6 * (1 - a), 1: 6 * (1 - a) + 1}
        sfx = str(a)
        return P, Q, M, bank, bcb, sfx

    def emit_p0_stage(q, only_h=None):
        bufx, xres, bufq, qres = p0bufs(q)
        hs = range(2) if only_h is None else (only_h,)
        for h in hs:
            for hf in range(2):
                def fbx(eng, h=h, hf=hf, bufx=bufx):
                    return eng.tensor_copy(bufx[h][:, hf * 2048:(hf + 1) * 2048],
                                           xbuf[h][:, hf * 2048:(hf + 1) * 2048])
                if xres[h] is CASTR:
                    wr = (CASTR[hf * 4:hf * 4 + 4])
                else:
                    wr = (xres[h][hf],) if len(xres[h]) == 2 else xres[h]
                step("dve", fbx, reads=(xr(h)[hf],), writes=wr)
        for h in hs:
            def fsq(eng, h=h, bufx=bufx, bufq=bufq):
                return eng.activation(bufq[h][:, :], bufx[h][:, :], AF.Square)
            step("act", fsq, reads=xres[h], writes=qres[h])

    def emit_p0_pair(q, drains=0):
        e_, o_ = 2 * q, 2 * q + 1
        bufx, xres, bufq, qres = p0bufs(q)
        P, Q, M, bank, bcb, sfx = p0slots(q)
        for h in range(2):
            for hf in range(2):
                def fred(eng, h=h, hf=hf):
                    ins = None
                    for c in range(hf * 4, hf * 4 + 4):
                        ins = eng.matmul(PB[bank[h]][0:1, :], onescb[:, :],
                                         bufx[h][:, c * TILE:(c + 1) * TILE],
                                         start=(c == 0), stop=(c == DC - 1))
                    return ins
                rr = (xres[h][hf],) if len(xres[h]) == 2 else xres[h]
                step("pe", fred, reads=rr + ("onescb",), writes=(f"pb{bank[h]}",))
        for h in range(2):
            def fred2(eng, h=h):
                ins = None
                for c in range(DC):
                    ins = eng.matmul(PB[bank[h]][32:33, :], onescb[:, :],
                                     bufq[h][:, c * TILE:(c + 1) * TILE],
                                     start=(c == 0), stop=(c == DC - 1))
                return ins
            step("pe", fred2, reads=qres[h] + ("onescb",), writes=(f"pb{bank[h]}",))
        drain_w(drains)
        for h in range(2):
            def fzc(eng, h=h):
                return eng.tensor_scalar(Q[h], PB[bank[h]][0:1, :], 1.0, 0.0,
                                         ALU.mult, ALU.add)
            step("dve", fzc, reads=(f"pb{bank[h]}",), writes=(f"p0Q{sfx}{h}",))
        for h in range(2):
            def fz(eng, h=h):
                return eng.scalar_tensor_tensor(P[h], Q[h], -1.0 / 1024.0,
                                                Q[h], ALU.mult, ALU.mult)
            step("dve", fz, reads=(f"p0Q{sfx}{h}",),
                 writes=(f"p0P{sfx}{h}", "th4row"), strict=True)
        for h in range(2):
            def fq(eng, h=h):
                return eng.scalar_tensor_tensor(P[h], P[h], 1.0,
                                                PB[bank[h]][32:33, :],
                                                ALU.mult, ALU.add)
            step("dve", fq, reads=(f"pb{bank[h]}", f"p0P{sfx}{h}"),
                 writes=(f"p0P{sfx}{h}", "th4row"), strict=True)
        for h in range(2):
            def fln(eng, h=h):
                return eng.activation(Q[h], P[h], AF.Ln,
                                      bias=eps_t[0:1, 0:1], scale=1.0 / 1024.0)
            step("act", fln, reads=(f"p0P{sfx}{h}", "eps_t"),
                 writes=(f"p0Q{sfx}{h}",))
        for h in range(2):
            def frr(eng, h=h):
                return eng.activation(P[h], Q[h], AF.Exp, scale=-0.5)
            step("act", frr, reads=(f"p0Q{sfx}{h}",),
                 writes=(f"p0P{sfx}{h}", "th4row"), strict=True)
        for h in range(2):
            def fmr(eng, h=h, j=(e_, o_)[h]):
                return eng.scalar_tensor_tensor(M[h], PB[bank[h]][0:1, :],
                                                1.0 / 1024.0, P[h],
                                                ALU.mult, ALU.mult,
                                                accum_out=mrs[0:1, j:j + 1])
            step("dve", fmr, reads=(f"pb{bank[h]}", f"p0P{sfx}{h}"),
                 writes=(f"p0M{sfx}{h}", "mrs0"))
        drain_w(drains)
        pp_ = 64 * (q % 2)
        for h in range(2):
            def fbc(eng, h=h, pp_=pp_):
                return eng.matmul(PB[bcb[h]][:, :], onesr[pp_:pp_ + 1, :], P[h],
                                  start=True, stop=True)
            step("pe", fbc, reads=(f"p0P{sfx}{h}", "th4row", "onesr"),
                 writes=(f"pb{bcb[h]}",))
        for h in range(2):
            def fxsum(eng, h=h, j=(e_, o_)[h]):
                ins = None
                for c in range(DC):
                    ins = eng.scalar_tensor_tensor(
                        tw0[:, :], bufx[h][:, c * TILE:(c + 1) * TILE],
                        1.0, PB[bcb[h]][:, :], ALU.mult, ALU.mult,
                        accum_out=xs_store[:, c * NT + j:c * NT + j + 1])
                return ins
            step("dve", fxsum, reads=xres[h] + (f"pb{bcb[h]}",),
                 writes=("tw0", f"xs_store{h}"))
        def fpk(eng, dsem, e_=e_, o_=o_):
            eng.dma_start(rmrp[:, e_ * 8:e_ * 8 + 4], P[0]).then_inc(dsem, 16)
            eng.dma_start(rmrp[:, e_ * 8 + 4:e_ * 8 + 8], M[0]).then_inc(dsem, 16)
            eng.dma_start(rmrp[:, o_ * 8:o_ * 8 + 4], P[1]).then_inc(dsem, 16)
            return eng.dma_start(rmrp[:, o_ * 8 + 4:o_ * 8 + 8],
                                 M[1]).then_inc(dsem, 16)
        step("dma", fpk, reads=(f"p0P{sfx}0", f"p0P{sfx}1", f"p0M{sfx}0",
                                f"p0M{sfx}1", "th4row"),
             writes=("rmrp",), n_dma=4, dcls="pack")
        drain_w(drains)

    def emit_gf1():
        def fmrs(eng):
            return eng.tensor_scalar(mrs[0:1, 0:8], mrs[0:1, 0:8], 1.0, 0.0, ALU.mult, ALU.add,
                                     accum_out=mrsR[0:1, :])
        step("dve", fmrs, reads=("mrs0",), writes=("mrs0", "mrsR"), strict=True)
        step("pe", lambda eng: eng.matmul(PB[3][:, 0:1], onesr[0:1, :], mrsR[0:1, :],
                                          start=True, stop=True),
             reads=("mrsR", "onesr"), writes=("pb3",))
        step("act", lambda eng: eng.activation(mrsb[:, :], PB[3][:, 0:1], AF.Identity),
             reads=("pb3",), writes=("mrsb",))
        def fxnm(eng):
            ins = None
            for c in range(DC):
                ins = eng.tensor_scalar(jr8[:, :], xs_store[:, c * NT:(c + 1) * NT], 1.0, 0.0,
                                        ALU.mult, ALU.add, accum_out=xnmf[:, c:c + 1])
            return ins
        step("dve", fxnm, reads=("xs_store0", "xs_store1"), writes=("jr8", "xnmf"), strict=True)
        def fxnm2(eng):
            return eng.tensor_scalar(xnmf[:, :], xnmf[:, :], mrsb[:, 0:1], 1.0 / 4096.0,
                                     ALU.subtract, ALU.mult)
        step("dve", fxnm2, reads=("xnmf", "mrsb"), writes=("xnmf",), strict=True)
        step("dve", lambda eng: eng.tensor_copy(xnmB[:, :], xnmf[:, :]),
             reads=("xnmf",), writes=("xnmB",), strict=True)
        def fm(eng):
            ins = None
            for k in range(8):
                for m in range(4):
                    ins = eng.matmul(PB[2 + m][:, 0:1],
                                     castb[:, k * 512 + m * 128:k * 512 + (m + 1) * 128],
                                     xnmB[:, k:k + 1], start=(k == 0), stop=(k == 7))
            return ins
        step("pe", fm, reads=tuple(f"castb{c}" for c in range(8)) + ("xnmB",),
             writes=("pb2", "pb3", "pb4", "pb5"))
        for m in range(4):
            def fe(eng, m=m):
                return eng.activation(hg1f[:, m:m + 1], PB[2 + m][:, 0:1], AF.Gelu,
                                      bias=bt["gb1"][:, m:m + 1])
            step("act", fe, reads=(f"pb{2 + m}", "b_gb1"), writes=("hg1f",))
        step("dve", lambda eng: eng.tensor_copy(hg1b[:, :], hg1f[:, :]),
             reads=("hg1f",), writes=("hg1b",))

    def emit_gf2():
        def fm(eng):
            ins = None
            for k in range(4):
                for m in range(2):
                    ins = eng.matmul(PB[6 + m][:, 0:1],
                                     h2b[:, k * 256 + m * 128:k * 256 + (m + 1) * 128],
                                     hg1b[:, k:k + 1], start=(k == 0), stop=(k == 3))
            return ins
        step("pe", fm, reads=("h2b", "hg1b"), writes=("pb6", "pb7"))
        for m in range(2):
            def fe(eng, m=m):
                return eng.activation(hg2f[:, m:m + 1], PB[6 + m][:, 0:1], AF.Gelu,
                                      bias=bt["gb2"][:, m:m + 1])
            step("act", fe, reads=(f"pb{6 + m}", "b_gb2"), writes=("hg2f",))
        step("dve", lambda eng: eng.tensor_copy(hg2b[:, :], hg2f[:, :]),
             reads=("hg2f",), writes=("hg2b",))

    def emit_gf3():
        def fm(eng):
            ins = None
            for k in range(2):
                ins = eng.matmul(PB[2][0:1, 0:1], g3B[:, k:k + 1], hg2b[:, k:k + 1],
                                 start=(k == 0), stop=(k == 1))
            return ins
        step("pe", fm, reads=("g3B", "hg2b"), writes=("pb2",))
        step("act", lambda eng: eng.activation(th3[0:1, :], PB[2][0:1, 0:1], AF.Tanh,
                                               scale=0.5, bias=g3h[0:1, 0:1]),
             reads=("pb2", "g3h"), writes=("th3",))
        def fc01a(eng):
            return eng.tensor_tensor(C0[0:1, :], bt["ar"][0:1, :], th3[0:1, :], ALU.mult)
        step("dve", fc01a, reads=("b_ar", "th3"), writes=("C0",))
        def fc01b(eng):
            eng.tensor_scalar(C0[0:1, :], C0[0:1, :], 0.7, 0.1, ALU.mult, ALU.add)
            return eng.tensor_scalar(C1[0:1, :], bt["ar"][0:1, :], 0.3, None, ALU.mult)
        step("dve", fc01b, reads=("b_ar", "C0"), writes=("C0", "C1"), strict=True)

    def unpack_rows(j):
        if j < 2:
            def f(eng, j=j):
                eng.tensor_copy(r_row[0:1, :], th4row[0:1, j * TILE:(j + 1) * TILE])
                return eng.tensor_copy(mr_row[0:1, :],
                                       p0rows[0:1, (2 + j) * TILE:(3 + j) * TILE])
            step("dve", f, reads=(f"p0P0{j}", f"p0M0{j}", "th4row"),
                 writes=("r_row", "mr_row"), strict=True)
            return
        def f(eng, dsem, j=j):
            eng.dma_start(r_row[0:1, :], rmrp[:, j * 8:j * 8 + 4]).then_inc(dsem, 16)
            return eng.dma_start(mr_row[0:1, :], rmrp[:, j * 8 + 4:j * 8 + 8]).then_inc(dsem, 16)
        step("dma", f, reads=("rmrp",), writes=("r_row", "mr_row"), n_dma=2, dcls="pack")

    def bcast_rows(j):
        b1 = pbank()
        b2 = pbank()
        def f1(eng, b1=b1):
            return eng.matmul(PB[b1][:, :], onesr[0:1, :], r_row[0:1, :],
                              start=True, stop=True)
        step("pe", f1, reads=("r_row", "onesr"), writes=(f"pb{b1}",))
        def f2(eng, b2=b2):
            return eng.matmul(PB[b2][:, :], onesr[0:1, :], mr_row[0:1, :],
                              start=True, stop=True)
        step("pe", f2, reads=("mr_row", "onesr"), writes=(f"pb{b2}",))
        def f3(eng, b1=b1):
            return eng.activation(rb_sb[:, :], PB[b1][:, :], AF.Identity)
        step("act", f3, reads=(f"pb{b1}",), writes=("rb_sb",))
        def f4(eng, b2=b2):
            return eng.activation(mrb_sb[:, :], PB[b2][:, :], AF.Identity)
        step("act", f4, reads=(f"pb{b2}",), writes=("mrb_sb",))

    def emit_xn(j):
        pi = j % 2
        for c in range(DC):
            def f(eng, pi=pi, c=c):
                eng.tensor_tensor(thT[:, 0:TILE], xbuf[pi][:, c * TILE:(c + 1) * TILE],
                                  rb_sb[:, :], ALU.mult)
                return eng.scalar_tensor_tensor(state[pi][:, c * TILE:(c + 1) * TILE],
                                                thT[:, 0:TILE], 1.0, mrb_sb[:, :],
                                                ALU.mult, ALU.subtract)
            step("dve", f, reads=xr(pi) + ("rb_sb", "mrb_sb"),
                 writes=("thT", stchunk(pi, c)))

    def emit_xnb(j, eng_name="pool"):
        pi = j % 2
        for h in range(2):
            if eng_name == "act":
                def f(eng, pi=pi, h=h):
                    return eng.activation(xnb[pi][:, h * 2048:(h + 1) * 2048],
                                          state[pi][:, h * 2048:(h + 1) * 2048],
                                          AF.Identity)
            else:
                def f(eng, pi=pi, h=h):
                    return eng.tensor_copy(xnb[pi][:, h * 2048:(h + 1) * 2048],
                                           state[pi][:, h * 2048:(h + 1) * 2048])
            step(eng_name, f, reads=tuple(stchunk(pi, c) for c in range(h * 4, h * 4 + 4)),
                 writes=(f"xnb{pi}" + ("lo" if h == 0 else "hi",)[0],))

    HB = [2, 3, 0]
    hrot = [0]

    def hbank():
        hrot[0] = (hrot[0] + 1) % 3
        return HB[hrot[0]]

    PBK = [1, 5, 6, 7]
    prot = [0]

    def pbank():
        prot[0] = (prot[0] + 1) % 4
        return PBK[prot[0]]

    castms = {}
    h1ev = {}
    h2ev = {}

    def emit_h1(j, k, ms):
        pi = j % 2
        if k == 1:
            rhs, rres = xnb[pi], (f"xnb{pi}lo", f"xnb{pi}hi")
            korder = tuple(range(8))
        else:
            rhs, rres = castb, tuple(f"castb{c}" for c in range(8))
            korder = (0, 4, 1, 5, 2, 6, 3, 7)
        for m in ms:
            b = hbank()
            if k > 1 and m == 0:
                cw = dict(castms)
                def f(eng, m=m, b=b, rhs=rhs, cw=cw, korder=korder):
                    ins = None
                    for i, kk in enumerate(korder):
                        we, wv = cw[kk]
                        eng.wait_ge(sem[we], wv)
                        ins = eng.matmul(PB[b][:, :],
                                         W1B[:, kk * 1024 + m * 128:kk * 1024 + (m + 1) * 128],
                                         rhs[:, kk * TILE:(kk + 1) * TILE],
                                         start=(i == 0), stop=(i == 7))
                    return ins
                step("pe", f, reads=("Ww1",), lazy_reads=rres, writes=(f"pb{b}",))
            else:
                def f(eng, m=m, b=b, rhs=rhs, korder=korder):
                    ins = None
                    for i, kk in enumerate(korder):
                        ins = eng.matmul(PB[b][:, :],
                                         W1B[:, kk * 1024 + m * 128:kk * 1024 + (m + 1) * 128],
                                         rhs[:, kk * TILE:(kk + 1) * TILE],
                                         start=(i == 0), stop=(i == 7))
                    return ins
                step("pe", f, reads=rres + ("Ww1",), writes=(f"pb{b}",))
            def fe(eng, m=m, b=b):
                return eng.activation(h1b[:, m * TILE:(m + 1) * TILE], PB[b][:, :], AF.Gelu,
                                      bias=bt["b1c"][:, m:m + 1])
            step("act", fe, reads=(f"pb{b}", "b_b1c"), writes=("h1b",))
            h1ev[m] = count["act"]

    def emit_h23z(j, k, defer_cb=False):
        t4off = (j % 2) * TILE if defer_cb else 0
        for m in range(4):
            b = hbank()
            if m == 0:
                ev = dict(h1ev)
                def f(eng, m=m, b=b, ev=ev):
                    ins = None
                    for kk in range(8):
                        eng.wait_ge(sem["act"], ev[kk])
                        ins = eng.matmul(PB[b][:, :],
                                         W2B[:, kk * 512 + m * 128:kk * 512 + (m + 1) * 128],
                                         h1b[:, kk * TILE:(kk + 1) * TILE],
                                         start=(kk == 0), stop=(kk == 7))
                    return ins
                step("pe", f, reads=("Ww2",), lazy_reads=("h1b",), writes=(f"pb{b}",))
            else:
                def f(eng, m=m, b=b):
                    ins = None
                    for kk in range(8):
                        ins = eng.matmul(PB[b][:, :],
                                         W2B[:, kk * 512 + m * 128:kk * 512 + (m + 1) * 128],
                                         h1b[:, kk * TILE:(kk + 1) * TILE],
                                         start=(kk == 0), stop=(kk == 7))
                    return ins
                step("pe", f, reads=("h1b", "Ww2"), writes=(f"pb{b}",))
            def fe(eng, m=m, b=b):
                return eng.activation(h2b[:, m * TILE:(m + 1) * TILE], PB[b][:, :], AF.Gelu,
                                      bias=bt["b2c"][:, m:m + 1])
            step("act", fe, reads=(f"pb{b}", "b_b2c"), writes=("h2b",))
            h2ev[m] = count["act"]
        for m in range(2):
            b = hbank()
            if m == 0:
                ev = dict(h2ev)
                def f(eng, m=m, b=b, ev=ev):
                    ins = None
                    for kk in range(4):
                        eng.wait_ge(sem["act"], ev[kk])
                        ins = eng.matmul(PB[b][:, :],
                                         W3B[:, kk * 256 + m * 128:kk * 256 + (m + 1) * 128],
                                         h2b[:, kk * TILE:(kk + 1) * TILE],
                                         start=(kk == 0), stop=(kk == 3))
                    return ins
                step("pe", f, reads=("Ww3",), lazy_reads=("h2b",), writes=(f"pb{b}",))
            else:
                def f(eng, m=m, b=b):
                    ins = None
                    for kk in range(4):
                        ins = eng.matmul(PB[b][:, :],
                                         W3B[:, kk * 256 + m * 128:kk * 256 + (m + 1) * 128],
                                         h2b[:, kk * TILE:(kk + 1) * TILE],
                                         start=(kk == 0), stop=(kk == 3))
                    return ins
                step("pe", f, reads=("h2b", "Ww3"), writes=(f"pb{b}",))
            def fe(eng, m=m, b=b):
                return eng.activation(h1b[:, m * TILE:(m + 1) * TILE], PB[b][:, :], AF.Gelu,
                                      bias=bt["b3c"][:, m:m + 1])
            step("act", fe, reads=(f"pb{b}", "b_b3c"), writes=("h1b",))
            h2ev["h3%d" % m] = count["act"]
        b = hbank()
        ev0 = h2ev["h30"]
        ev1 = h2ev["h31"]
        def fz4(eng, b=b, ev0=ev0, ev1=ev1):
            ins = None
            for kk in range(2):
                eng.wait_ge(sem["act"], ev0 if kk == 0 else ev1)
                ins = eng.matmul(PB[b][0:1, :], W4B[:, kk:kk + 1],
                                 h1b[:, kk * TILE:(kk + 1) * TILE],
                                 start=(kk == 0), stop=(kk == 1))
            return ins
        step("pe", fz4, reads=("Ww4",), lazy_reads=("h1b",), writes=(f"pb{b}",))
        def fth4(eng, b=b, t4off=t4off):
            return eng.activation(th4row[0:1, t4off:t4off + TILE], PB[b][0:1, :], AF.Tanh,
                                  scale=0.5, bias=b4h[0:1, 0:1])
        step("act", fth4, reads=(f"pb{b}", "b4h"), writes=("th4row",))
        if not defer_cb:
            emit_fcb(0)

    def emit_fcb(t4off):
        def fcb(eng, t4off=t4off):
            return eng.activation(cbrow[0:1, :], th4row[0:1, t4off:t4off + TILE],
                                  AF.Identity, scale=C1[0:1, 0:1], bias=C0[0:1, 0:1])
        step("act", fcb, reads=("th4row", "C1", "C0"), writes=("cbrow",), strict=True)

    def emit_cb(j, k):
        def fbc(eng):
            return eng.matmul(PB[4][:, :], onesr[0:1, :], cbrow[0:1, :], start=True, stop=True)
        step("pe", fbc, reads=("cbrow", "onesr"), writes=("pb4",))
        return 4

    def emit_upd_pair(j, cb, c):
        pi = j % 2
        sA = ALU.add if c < 2 else ALU.subtract
        sB = ALU.subtract if c < 2 else ALU.add
        def f(eng, pi=pi, c=c, sA=sA, sB=sB, cb=cb):
            cA = state[pi][:, c * TILE:(c + 1) * TILE]
            cB = state[pi][:, (c + 4) * TILE:(c + 5) * TILE]
            eng.tensor_tensor(tw0[:, :], PB[cb][:, :], cB, ALU.mult)
            eng.tensor_tensor(tw1[:, :], PB[cb][:, :], cA, ALU.mult)
            eng.tensor_tensor(cA, cA, tw0[:, :], sA)
            return eng.tensor_tensor(cB, cB, tw1[:, :], sB)
        step("dve", f, reads=(f"pb{cb}", stchunk(pi, c), stchunk(pi, c + 4)),
             writes=(stchunk(pi, c), stchunk(pi, c + 4), "tw0", "tw1"))

    def emit_upd_casts(j, cb, weave=None):
        weave = list(weave or [])
        pi = j % 2
        def fp0a(eng, pi=pi, cb=cb):
            cA = state[pi][:, 0:TILE]
            cB = state[pi][:, 4 * TILE:5 * TILE]
            eng.tensor_tensor(tw0[:, :], PB[cb][:, :], cB, ALU.mult)
            eng.tensor_tensor(tw1[:, :], PB[cb][:, :], cA, ALU.mult)
            eng.tensor_tensor(cA, cA, tw0[:, :], ALU.add)
            return eng.tensor_copy(castb[:, 0:TILE], cA)
        step("dve", fp0a, reads=(f"pb{cb}", stchunk(pi, 0), stchunk(pi, 4)),
             writes=(stchunk(pi, 0), "tw0", "tw1", "castb0"))
        castms[0] = ("dve", count["dve"])
        def fp0b(eng, pi=pi):
            cB = state[pi][:, 4 * TILE:5 * TILE]
            eng.tensor_tensor(cB, cB, tw1[:, :], ALU.subtract)
            return eng.tensor_copy(castb[:, 4 * TILE:5 * TILE], cB)
        step("dve", fp0b, reads=("tw1", stchunk(pi, 4)),
             writes=(stchunk(pi, 4), "castb4"))
        castms[4] = ("dve", count["dve"])
        for c in range(1, 4):
            emit_upd_pair(j, cb, c)
            for cc in (c, c + 4):
                def fcst(eng, pi=pi, cc=cc):
                    return eng.activation(castb[:, cc * TILE:(cc + 1) * TILE],
                                          state[pi][:, cc * TILE:(cc + 1) * TILE],
                                          AF.Identity)
                step("act", fcst, reads=(stchunk(pi, cc),), writes=(f"castb{cc}",))
                castms[cc] = ("act", count["act"])
                if weave:
                    emit_proj_evac(*weave.pop(0))
        while weave:
            emit_proj_evac(*weave.pop(0))

    def emit_proj_mm(j, which):
        pi = j % 2
        wmap = {"a": (WaB, "WWa"), "b": (WbB, "WWb"), "c": (WcB, "WWc")}
        WB, wres = wmap[which]
        out = []
        for n in range(NC):
            b = pbank()
            def f(eng, n=n, b=b, pi=pi, WB=WB):
                ins = None
                for kk in range(8):
                    ins = eng.matmul(PB[b][:, :],
                                     WB[:, kk * 256 + n * 128:kk * 256 + (n + 1) * 128],
                                     xnb[pi][:, kk * TILE:(kk + 1) * TILE],
                                     start=(kk == 0), stop=(kk == 7))
                return ins
            step("pe", f, reads=(f"xnb{pi}lo", f"xnb{pi}hi", wres), writes=(f"pb{b}",))
            out.append((which, n, b))
        return out

    def emit_proj_evac(which, n, b):
        if which == "a":
            def fe(eng, n=n, b=b):
                return eng.activation(thT[:, n * TILE:(n + 1) * TILE], PB[b][:, :], AF.Tanh,
                                      scale=0.5, bias=ba_h[:, n:n + 1])
            step("act", fe, reads=(f"pb{b}", "ba_h"), writes=("thT",))
        elif which == "b":
            def fe(eng, n=n, b=b):
                return eng.activation(buT[:, n * TILE:(n + 1) * TILE], PB[b][:, :],
                                      AF.Identity, bias=bt["bb2"][:, n:n + 1])
            step("act", fe, reads=(f"pb{b}", "b_bb2"), writes=("buT",))
        else:
            def fe(eng, n=n, b=b):
                return eng.activation(cT[:, n * TILE:(n + 1) * TILE], PB[b][:, :], AF.Tanh,
                                      bias=bt["bc2"][:, n:n + 1])
            step("act", fe, reads=(f"pb{b}", "b_bc2"), writes=("cT",))

    def emit_proj(j, which):
        for which_, n, b in emit_proj_mm(j, which):
            emit_proj_evac(which_, n, b)

    def emit_scan(j):
        def f1(eng):
            eng.tensor_scalar(aT[:, :], thT[:, :], 0.5, 0.5, ALU.mult, ALU.add)
            return eng.tensor_scalar(thT[:, :], thT[:, :], -0.5, 0.5, ALU.mult, ALU.add)
        step("dve", f1, reads=("thT",), writes=("aT", "thT"))
        def f2(eng):
            return eng.tensor_tensor(buT[:, :], thT[:, :], buT[:, :], ALU.mult)
        step("dve", f2, reads=("thT", "buT"), writes=("buT",))
        for n in range(NC):
            def f3(eng, n=n):
                return eng.tensor_tensor_scan(buT[:, n * TILE:(n + 1) * TILE],
                                              aT[:, n * TILE:(n + 1) * TILE],
                                              buT[:, n * TILE:(n + 1) * TILE],
                                              carry[:, n:n + 1], ALU.mult, ALU.add)
            step("dve", f3, reads=("aT", "buT", "carry"), writes=("buT",))
        def f4(eng):
            ins = None
            for n in range(NC):
                ins = eng.tensor_copy(carry[:, n:n + 1],
                                      buT[:, n * TILE + TILE - 1:n * TILE + TILE])
            return ins
        step("dve", f4, reads=("buT",), writes=("carry",), strict=True)

    def emit_cs(j):
        def f(eng):
            return eng.tensor_tensor(csT[:, :], cT[:, :], buT[:, :], ALU.mult)
        step("dve", f, reads=("cT", "buT"), writes=("csT",))

    ybank = {}

    def emit_ygroup(j, m):
        pi = j % 2
        b = pbank()
        ybank[(j, m)] = b
        def fy(eng, m=m, b=b, pi=pi):
            ins = None
            for kk in range(8):
                ins = eng.matmul(PB[b][:, :],
                                 WdB[:, kk * 1024 + m * 128:kk * 1024 + (m + 1) * 128],
                                 xnb[pi][:, kk * TILE:(kk + 1) * TILE],
                                 start=(kk == 0), stop=False)
            for kk in range(2):
                ins = eng.matmul(PB[b][:, :],
                                 WyB[:, kk * 1024 + m * 128:kk * 1024 + (m + 1) * 128],
                                 csT[:, kk * TILE:(kk + 1) * TILE],
                                 start=False, stop=(kk == 1))
            return ins
        step("pe", fy, reads=(f"xnb{pi}lo", f"xnb{pi}hi", "csT", "WWd", "WWy"),
             writes=(f"pb{b}",))

    def emit_pfold(j, m):
        pi = j % 2
        b = ybank[(j, m)]
        def fp(eng, m=m, b=b):
            return eng.scalar_tensor_tensor(gbuf[:, m * TILE:(m + 1) * TILE], PB[b][:, :],
                                            byd[:, m:m + 1],
                                            gbuf[:, m * TILE:(m + 1) * TILE],
                                            ALU.add, ALU.mult)
        step("dve", fp, reads=(f"pb{b}", "byd", f"g{m}"), writes=(f"g{m}",))
        def fo(eng, c=m, pi=pi):
            return eng.tensor_tensor(xbuf[pi][:, c * TILE:(c + 1) * TILE],
                                     gbuf[:, c * TILE:(c + 1) * TILE],
                                     xbuf[pi][:, c * TILE:(c + 1) * TILE], ALU.add)
        step("dve", fo, reads=(f"g{m}", xr(pi)[m // 4]), writes=(xr(pi)[m // 4],))

    def emit_silu_chunk(j, cc):
        pi = j % 2
        def fs(eng, pi=pi, cc=cc):
            return eng.activation(gbuf[:, cc * TILE:(cc + 1) * TILE],
                                  state[pi][:, cc * TILE:(cc + 1) * TILE], AF.Silu)
        step("act", fs, reads=(stchunk(pi, cc),), writes=(f"g{cc}",))

    def emit_outdma(j, h):
        pi = j % 2
        def fo(eng, dsem, pi=pi, j=j, h=h):
            dst = bass.AP(oT, j * TILE + h * 4 * 128 * T,
                          [[T, 128], [128 * T, DC // 2], [1, TILE]])
            return eng.dma_start(dst, xbuf[pi][:, h * 2048:(h + 1) * 2048],
                                 ).then_inc(dsem, 16)
        step("dma", fo, reads=(xr(pi)[h],), n_dma=1, dcls="out")

    load_x(0)
    load_x(1)
    emit_p0_stage(0)
    step("dma", fbias, writes=tuple("b_" + n for n in db), n_dma=len(db), dcls="bias")
    drain_w(2)
    emit_p0_pair(0, drains=2)
    step("dve", bias_prep,
         reads=("b_ba2", "b_b4", "b_gb3", "b_bd8", "b_by8"),
         writes=("ba_h", "b4h", "g3h", "byd"))
    unpack_rows(0)
    bcast_rows(0)
    emit_xn(0)
    if debug:
        dbg("xn0", state[0][:, :], [stchunk(0, c) for c in range(8)])
    load_x(2)
    emit_p0_stage(1, only_h=0)
    unpack_rows(1)
    bcast_rows(1)
    emit_xn(1)
    load_x(3)
    emit_p0_stage(1, only_h=1)
    load_x(4)
    load_x(5)
    emit_p0_pair(1, drains=2)
    emit_p0_stage(2)
    load_x(6)
    load_x(7)
    emit_p0_pair(2, drains=2)
    emit_p0_stage(3)
    emit_p0_pair(3, drains=2)
    if debug:
        dbg("rmrp", rmrp[:, :], ["rmrp"])
        dbg("mrs", mrs[0:1, :], ["mrs0"])
    emit_xnb(0, "act")
    drain_w(4)
    need_w("w1")
    emit_h1(0, 1, range(8))
    allow_g[0] = True
    need_w("g1")
    need_w("w2")
    need_w("w3")
    need_w("w4")
    emit_h23z(0, 1, defer_cb=True)
    emit_xnb(1, "act")
    need_w("g2")
    need_w("g3")
    if debug:
        dbg("th4row0", th4row[0:1, 0:TILE], ["th4row"])
    need_w("Wa")
    emit_proj(0, "a")
    need_w("Wb")
    if debug:
        dbg("WbEarly", WbB[:, :], ["WWb"])
    emit_proj(0, "b")
    need_w("Wc")
    emit_proj(0, "c")
    if debug:
        dbg("thT0", thT[:, :], ["thT"])
        dbg("buT0", buT[:, :], ["buT"])
    drain_w(10 ** 9)
    drain_cv(10 ** 9, force=True)
    load_x(0)
    load_x(1)
    if debug:
        dbg("WaBd", WaB[:, :], ["WWa"])
        dbg("WbBd", WbB[:, :], ["WWb"])
        dbg("WcBd", WcB[:, :], ["WWc"])
        dbg("W4Bd", W4B[:, :], ["Ww4"])
        dbg("W1Bd", W1B[:, :], ["Ww1"])
        dbg("g1Bd", castb[:, :], list(CASTR))
        dbg("g2Bd", h2b[:, 0:1024], ["h2b"])
        dbg("g3Bd", g3B[:, :], ["g3B"])

    emit_gf1()
    if debug:
        dbg("hg1f", hg1f[:, :], ["hg1f"])
    emit_gf2()
    if debug:
        dbg("hg2f", hg2f[:, :], ["hg2f"])
    emit_gf3()
    if debug:
        dbg("xnmf", xnmf[:, :], ["xnmf"])
        dbg("C0", C0[0:1, :], ["C0"])
        dbg("C1", C1[0:1, :], ["C1"])
        dbg("th3", th3[0:1, :], ["th3"])

    def emit_tile_ffn_tail(j, weave_next, lead_full=True):
        emit_h1(j, 2, range(8))
        emit_h23z(j, 2)
        if weave_next and lead_full:
            bcast_rows(j + 1)
        emit_ygroup(j, 4)
        cb = emit_cb(j, 2)
        emit_upd_casts(j, cb)
        if weave_next and lead_full:
            emit_xn(j + 1)
        emit_h1(j, 3, range(8))
        emit_h23z(j, 3)
        emit_ygroup(j, 0)
        if j == 0 and debug:
            dbg("state0c2", state[0][:, :], [stchunk(0, c) for c in range(8)])
        cb = emit_cb(j, 3)
        emit_upd_pair(j, cb, 0)
        emit_silu_chunk(j, 0)
        emit_silu_chunk(j, 4)
        emit_ygroup(j, 1)
        emit_pfold(j, 0)
        emit_pfold(j, 4)
        if weave_next:
            if lead_full:
                emit_xnb(j + 1)
            emit_h1(j + 1, 1, range(0, 4))
        emit_upd_pair(j, cb, 1)
        emit_silu_chunk(j, 1)
        emit_silu_chunk(j, 5)
        emit_ygroup(j, 5)
        emit_pfold(j, 1)
        emit_pfold(j, 5)
        if weave_next:
            emit_h1(j + 1, 1, range(4, 8))
        emit_upd_pair(j, cb, 2)
        emit_silu_chunk(j, 2)
        emit_silu_chunk(j, 6)
        emit_ygroup(j, 2)
        emit_pfold(j, 2)
        emit_upd_pair(j, cb, 3)
        emit_silu_chunk(j, 3)
        emit_silu_chunk(j, 7)
        emit_ygroup(j, 6)
        emit_pfold(j, 6)
        emit_ygroup(j, 3)
        emit_pfold(j, 3)
        emit_outdma(j, 0)
        emit_ygroup(j, 7)
        emit_pfold(j, 7)
        if j == 0 and debug:
            dbg("p0", gbuf[:, :], [f"g{c}" for c in range(8)])
            dbg("out0", xbuf[0][:, :], ["xbuf0lo", "xbuf0hi"])
        emit_outdma(j, 1)

    emit_scan(0)
    emit_cs(0)
    if debug:
        dbg("aT0", aT[:, :], ["aT"])
        dbg("s0", buT[:, :], ["buT"])
        dbg("cT0", cT[:, :], ["cT"])
        dbg("cs0", csT[:, :], ["csT"])

    emit_fcb(0)
    cb = emit_cb(0, 1)
    emit_upd_casts(0, cb)
    if debug:
        dbg("state0c1", state[0][:, :], [stchunk(0, c) for c in range(8)])
    emit_tile_ffn_tail(0, weave_next=True, lead_full=False)
    load_x(2)
    unpack_rows(2)

    for j in range(1, NT):
        emit_h23z(j, 1)
        emit_proj(j, "a")
        cb = emit_cb(j, 1)
        emit_proj(j, "b")
        emit_proj(j, "c")
        emit_upd_casts(j, cb)
        emit_scan(j)
        emit_cs(j)
        emit_tile_ffn_tail(j, weave_next=(j + 1 < NT))
        if j + 2 < NT:
            load_x(j + 2)
            unpack_rows(j + 2)

    per_eng = {k: [] for k in ["pe", "act", "dve", "pool", "dma"]}
    for s in steps:
        per_eng[s["eng"]].append(s)

    def emit_for(eng_name):
        def run(engobj):
            maxw = {}
            for s in per_eng[eng_name]:
                for d in sorted(s["deps"]):
                    ps = steps[d]
                    if (ps["eng"] == eng_name and eng_name != "dma"
                            and not ps["is_dma"] and not s["strict"]):
                        continue
                    k = ps["key"]
                    val = ps["val"]
                    if maxw.get(k, -1) >= val:
                        continue
                    engobj.wait_ge(sem[k], val)
                    maxw[k] = val
                if s["is_dma"]:
                    s["fn"](engobj, sem[s["key"]])
                else:
                    last = s["fn"](engobj)
                    last.then_inc(sem[s["eng"]], 1)
            if eng_name == "dma":
                for k in DCLS:
                    if count[k]:
                        engobj.wait_ge(sem[k], count[k])
        return run

    with nc.Block() as block:
        @block.tensor
        def _(tensor):
            emit_for("pe")(tensor)

        @block.scalar
        def _(scalar):
            emit_for("act")(scalar)

        @block.vector
        def _(vector):
            emit_for("dve")(vector)

        @block.gpsimd
        def _(gpsimd):
            emit_for("pool")(gpsimd)

        @block.sync
        def _(sync):
            emit_for("dma")(sync)

    ctx.close()
    nc._dbg = dbg_tensors
    return nc


def _cvec(v, ncols):
    return np.ascontiguousarray(np.asarray(v, np.float32).reshape(ncols, 128).T)


def kernel(**inputs):
    g = lambda k: np.asarray(inputs[k], np.float32)
    x = g("L0_attn_x")
    base = {
        "Wa": g("L0_attn_Wa"), "Wb": g("L0_attn_Wb"), "Wc": g("L0_attn_Wc"),
        "Wd": g("L0_attn_Wd"), "Wy": g("L0_attn_Wy"),
        "w1": g("L0_ffn_cn_w1"), "w2": g("L0_ffn_cn_w2"), "w3": g("L0_ffn_cn_w3"),
        "w4": g("L0_ffn_cn_w4"),
        "g1": g("L0_ffn_gc_w1"), "g2": g("L0_ffn_gc_w2"), "g3": g("L0_ffn_gc_w3"),
        "ba2": _cvec(g("L0_attn_ba"), 2), "bb2": _cvec(g("L0_attn_bb"), 2),
        "bc2": _cvec(g("L0_attn_bc"), 2),
        "bd8": _cvec(g("L0_attn_bd"), 8), "by8": _cvec(g("L0_attn_by"), 8),
        "b1c": _cvec(g("L0_ffn_cn_b1"), 8), "b2c": _cvec(g("L0_ffn_cn_b2"), 4),
        "b3c": _cvec(g("L0_ffn_cn_b3"), 2),
        "b4": g("L0_ffn_cn_b4").reshape(1, 1),
        "gb1": _cvec(g("L0_ffn_gc_b1"), 4), "gb2": _cvec(g("L0_ffn_gc_b2"), 2),
        "gb3": g("L0_ffn_gc_b3").reshape(1, 1),
        "ar": g("L0_ffn_adaptive_range").reshape(1, 1),
    }
    base = {k: np.ascontiguousarray(v) for k, v in base.items()}
    in_maps = []
    for b in range(8):
        m = dict(base)
        m["xT"] = np.ascontiguousarray(x[b].T)
        in_maps.append(m)
    if "nc" not in _NC_CACHE:
        _NC_CACHE["nc"] = build_nc()
    res = run_bass_kernel_spmd(_NC_CACHE["nc"], in_maps, core_ids=list(range(8)))
    out = np.stack([np.asarray(res.results[b]["oT"]).T for b in range(8)])
    return np.ascontiguousarray(out.astype(np.float32))


if __name__ == "__main__":
    nc = build_nc()
    print("built ok")


# version 87
# speedup vs baseline: 1.0592x; 1.0111x over previous
import numpy as np
import concourse.bass as bass
import concourse.mybir as mybir
from concourse.bass_utils import run_bass_kernel_spmd
from contextlib import ExitStack

F32 = mybir.dt.float32
BF16 = mybir.dt.bfloat16
AF = mybir.ActivationFunctionType
ALU = mybir.AluOpType

D = 1024
T = 4096
NS = 256
TILE = 512
NT = T // TILE
DC = D // 128
NC = NS // 128
EPS = 1e-5

_NC_CACHE = {}


def build_nc(debug=False):
    nc = bass.Bass("TRN2", target_bir_lowering=False, dynamic_dma_scratch_size=1024)

    xT = nc.dram_tensor("xT", [D, T], F32, kind="ExternalInput")
    oT = nc.dram_tensor("oT", [D, T], F32, kind="ExternalOutput")
    dw = {}
    for name, shp in [("Wa", [D, NS]), ("Wb", [D, NS]), ("Wc", [D, NS]),
                      ("Wd", [D, D]), ("Wy", [NS, D]),
                      ("w1", [D, D]), ("w2", [D, 512]), ("w3", [512, 256]), ("w4", [256, 1]),
                      ("g1", [D, 512]), ("g2", [512, 256]), ("g3", [256, 1])]:
        dw[name] = nc.dram_tensor(name, shp, F32, kind="ExternalInput")
    bspec = [("ba2", [128, 2]), ("bb2", [128, 2]), ("bc2", [128, 2]),
             ("bd8", [128, 8]), ("by8", [128, 8]),
             ("b1c", [128, 8]), ("b2c", [128, 4]), ("b3c", [128, 2]), ("b4", [1, 1]),
             ("gb1", [128, 4]), ("gb2", [128, 2]), ("gb3", [1, 1]), ("ar", [1, 1])]
    db = {name: nc.dram_tensor(name, shp, F32, kind="ExternalInput") for name, shp in bspec}

    ctx = ExitStack()
    e = ctx.enter_context

    WaB = e(nc.sbuf_tensor("WaB", [128, 8 * 256], BF16))
    WbB = e(nc.sbuf_tensor("WbB", [128, 8 * 256], BF16))
    WcB = e(nc.sbuf_tensor("WcB", [128, 8 * 256], BF16))
    WdB = e(nc.sbuf_tensor("WdB", [128, 8 * 1024], BF16))
    WyB = e(nc.sbuf_tensor("WyB", [128, 2 * 1024], BF16))
    W1B = e(nc.sbuf_tensor("W1B", [128, 8 * 1024], BF16))
    W2B = e(nc.sbuf_tensor("W2B", [128, 8 * 512], BF16))
    W3B = e(nc.sbuf_tensor("W3B", [128, 4 * 256], BF16))
    W4B = e(nc.sbuf_tensor("W4B", [128, 2 * 1], BF16))
    bt = {name: e(nc.sbuf_tensor("t_" + name, shp, F32)) for name, shp in bspec}
    ba_h = e(nc.sbuf_tensor("ba_h", [128, 2], F32))
    b4h = e(nc.sbuf_tensor("b4h", [1, 1], F32))
    g3h = e(nc.sbuf_tensor("g3h", [1, 1], F32))
    byd = e(nc.sbuf_tensor("byd", [128, 8], F32))
    onesr = e(nc.sbuf_tensor("onesr", [65, 128], F32))
    onescb = e(nc.sbuf_tensor("onescb", [128, 1], BF16))
    rmrp = e(nc.sbuf_tensor("rmrp", [128, NT * 8], F32))
    xs_store = e(nc.sbuf_tensor("xs_store", [128, DC * NT], F32))
    mrs = e(nc.sbuf_tensor("mrs", [1, NT], F32))
    mrsR = e(nc.sbuf_tensor("mrsR", [1, 1], F32))
    mrsb = e(nc.sbuf_tensor("mrsb", [128, 1], F32))
    xnmf = e(nc.sbuf_tensor("xnmf", [128, DC], F32))
    jr8 = e(nc.sbuf_tensor("jr8", [128, 8], F32))
    hg1f = e(nc.sbuf_tensor("hg1f", [128, 4], F32))
    hg2f = e(nc.sbuf_tensor("hg2f", [128, 2], F32))
    th3 = e(nc.sbuf_tensor("th3", [1, 1], F32))
    C0 = e(nc.sbuf_tensor("C0", [1, 1], F32))
    C1 = e(nc.sbuf_tensor("C1", [1, 1], F32))
    th4row = e(nc.sbuf_tensor("th4row", [65, 2 * TILE], F32))
    cbrow = e(nc.sbuf_tensor("cbrow", [1, TILE], F32))
    r_row = e(nc.sbuf_tensor("r_row", [1, TILE], F32))
    mr_row = e(nc.sbuf_tensor("mr_row", [1, TILE], F32))
    carry = e(nc.sbuf_tensor("carry", [128, NC], F32))
    eps_t = e(nc.sbuf_tensor("eps_t", [1, 1], F32))
    p0rows = e(nc.sbuf_tensor("p0rows", [65, 4 * TILE], F32))

    xbuf = [e(nc.sbuf_tensor(f"xbuf{p}", [128, DC * TILE], F32)) for p in range(2)]
    xnb = [e(nc.sbuf_tensor(f"xnb{p}", [128, DC * TILE], BF16)) for p in range(2)]
    castb = e(nc.sbuf_tensor("castb", [128, DC * TILE], BF16))
    state = [e(nc.sbuf_tensor(f"state{p}", [128, DC * TILE], F32)) for p in range(2)]
    gbuf = e(nc.sbuf_tensor("gbuf", [128, DC * TILE], F32))
    h1b = e(nc.sbuf_tensor("h1b", [128, 8 * TILE], BF16))
    h2b = e(nc.sbuf_tensor("h2b", [128, 4 * TILE], BF16))
    tw0 = e(nc.sbuf_tensor("tw0", [128, TILE], F32))
    tw1 = e(nc.sbuf_tensor("tw1", [128, TILE], F32))
    thT = e(nc.sbuf_tensor("thT", [128, NC * TILE], F32))
    aT = e(nc.sbuf_tensor("aT", [128, NC * TILE], F32))
    buT = e(nc.sbuf_tensor("buT", [128, NC * TILE], F32))
    cT = e(nc.sbuf_tensor("cT", [128, NC * TILE], F32))
    csT = e(nc.sbuf_tensor("csT", [128, NC * TILE], BF16))
    g3B = e(nc.sbuf_tensor("g3B", [128, 2], BF16))
    xnmB = e(nc.sbuf_tensor("xnmB", [128, DC], BF16))
    hg1b = e(nc.sbuf_tensor("hg1b", [128, 4], BF16))
    hg2b = e(nc.sbuf_tensor("hg2b", [128, 2], BF16))
    rb_sb = e(nc.sbuf_tensor("rb_sb", [128, TILE], F32))
    mrb_sb = e(nc.sbuf_tensor("mrb_sb", [128, TILE], F32))

    rem = nc.sbuf_bytes_remaining
    assert rem >= 0, f"SBUF over budget by {-rem} bytes"
    import sys
    print(f"[build] sbuf_bytes_remaining={rem}", file=sys.stderr)

    PB = [e(nc.psum_tensor(f"PB{i}", [128, TILE], F32)) for i in range(8)]
    DCLS = ["bias", "wst0", "wst1", "wst2", "wst3", "x0", "x1", "x0h", "x1h",
            "pack", "out", "dbg"]
    sem = {k: e(nc.semaphore("sem_" + k)) for k in ["pe", "act", "dve", "pool"] + DCLS}

    dbg_tensors = {}

    steps = []
    count = {k: 0 for k in ["pe", "act", "dve", "pool"] + DCLS}
    last_w = {}
    readers = {}

    def step(eng, fn, reads=(), writes=(), n_dma=0, strict=False, dcls=None,
             lazy_reads=()):
        deps = set()
        for r in reads:
            if r in last_w:
                deps.add(last_w[r])
        for r in writes:
            deps |= readers.get(r, set())
            if r in last_w:
                deps.add(last_w[r])
        sid = len(steps)
        is_dma = n_dma > 0
        key = dcls if is_dma else eng
        if is_dma:
            count[key] += 16 * n_dma
        else:
            count[key] += 1
        steps.append(dict(eng=eng, fn=fn, deps=deps, val=count[key], strict=strict,
                          key=key, is_dma=is_dma))
        for r in reads:
            readers.setdefault(r, set()).add(sid)
        for r in lazy_reads:
            readers.setdefault(r, set()).add(sid)
        for r in writes:
            last_w[r] = sid
            readers[r] = {sid}
        return sid

    def stchunk(pi, c):
        return f"st{pi}c{c}"

    def dbg(name, ap, reads):
        if not debug:
            return
        shp = [ap.shape[0], int(np.prod(ap.shape[1:]))]
        dt_ = nc.dram_tensor("dbg_" + name, shp, ap.dtype, kind="ExternalOutput")
        dbg_tensors[name] = dt_
        def f(eng, dsem, dt_=dt_, ap=ap):
            return eng.dma_start(dt_[:, :], ap).then_inc(dsem, 16)
        step("dma", f, reads=tuple(reads), n_dma=1, dcls="dbg")

    def consts_prep(eng):
        eng.memset(onesr[:, :], 1.0)
        eng.memset(eps_t[:, :], EPS)
        eng.memset(carry[:, :], 0.0)
        return eng.memset(onescb[:, :], 1.0)
    step("dve", consts_prep, writes=("onesr", "eps_t", "onescb", "carry"))

    def fbias(eng, dsem):
        ins = None
        for name in db:
            ins = eng.dma_start(bt[name][:, :], db[name][:, :]).then_inc(dsem, 16)
        return ins

    def bias_prep(eng):
        eng.tensor_scalar(ba_h[:, :], bt["ba2"][:, :], 0.5, None, ALU.mult)
        eng.tensor_scalar(b4h[:, :], bt["b4"][:, :], 0.5, None, ALU.mult)
        eng.tensor_scalar(g3h[:, :], bt["gb3"][:, :], 0.5, None, ALU.mult)
        return eng.tensor_tensor(byd[:, :], bt["bd8"][:, :], bt["by8"][:, :], ALU.add)

    wspec = [("w1", W1B, 8, 1024), ("w2", W2B, 8, 512), ("w3", W3B, 4, 256), ("w4", W4B, 2, 1),
             ("Wa", WaB, 8, 256), ("Wb", WbB, 8, 256), ("Wc", WcB, 8, 256),
             ("Wd", WdB, 8, 1024), ("Wy", WyB, 2, 1024)]
    bigbufs = [(gbuf, 0, tuple(f"wstg0c{c}" for c in range(4)),
                tuple(f"g{c}" for c in range(4))),
               (gbuf, 2048, tuple(f"wstg1c{c}" for c in range(4)),
                tuple(f"g{c}" for c in range(4, 8)))]
    wpieces = []
    for name, wb, nk, M in wspec:
        nh = max(1, M // TILE)
        for k in range(nk):
            for h in range(nh):
                wpieces.append((name, wb, M, k, h * TILE, min(M, h * TILE + TILE),
                                ("W" + name,)))
    for k in range(8):
        wpieces.append(("g1", castb, 512, k, 0, 512, (f"castb{k}",)))
    for k in range(4):
        wpieces.append(("g2", h2b, 256, k, 0, 256, ("h2b",)))
    for k in range(2):
        wpieces.append(("g3", g3B, 1, k, 0, 1, ("g3B",)))
    bigjobs = []
    bi = 0
    while wpieces:
        grp = [wpieces.pop(0)]
        while wpieces and len(grp) < 4 and wpieces[0][0] == grp[0][0]:
            grp.append(wpieces.pop(0))
        bigjobs.append((grp, bi % 2))
        bi += 1

    pend_cv = []
    wround = [0]

    def emit_cvs(cvs):
        for eng_name, fcp, bres, wres in cvs:
            step(eng_name, fcp, reads=bres, writes=wres)

    def drain_cv(njobs, force=False):
        while pend_cv and njobs > 0:
            if not force and pend_cv[0][2] > wround[0] - 2:
                break
            emit_cvs(pend_cv.pop(0)[3])
            njobs -= 1

    allow_g = [False]

    def drain_w(n):
        if n <= 0:
            return
        wround[0] += 1
        drain_cv(n)
        while bigjobs and n > 0:
            if not allow_g[0] and bigjobs[0][0][0][0] in ("g1", "g2", "g3"):
                break
            grp, p = bigjobs.pop(0)
            for _i, ent in enumerate(pend_cv):
                if ent[0] == p:
                    emit_cvs(ent[3])
                    pend_cv.pop(_i)
                    break
            buf, boff, bres, gres = bigbufs[p]
            name = grp[0][0]
            d = dw[name]
            M = grp[0][2]
            npc = len(grp)
            k0, w00 = grp[0][3], grp[0][4]
            pw = grp[0][5] - grp[0][4]
            if M <= TILE:
                def fl(eng, dsem, d=d, buf=buf, boff=boff, k0=k0, npc=npc, pw=pw, M=M):
                    srcap = bass.AP(d, k0 * 128 * M, [[M, 128], [128 * M, npc], [1, pw]])
                    dstap = bass.AP(buf, boff, [[DC * TILE, 128], [TILE, npc], [1, pw]])
                    return eng.dma_start(dstap, srcap,
                                         allow_slow_non_contiguous=(pw < 16)).then_inc(dsem, 16)
            else:
                assert npc == 4 and w00 == 0 and M == 2 * TILE
                def fl(eng, dsem, d=d, buf=buf, boff=boff, k0=k0, M=M):
                    srcap = bass.AP(d, k0 * 128 * M, [[M, 128], [128 * M, 2], [1, M]])
                    dstap = bass.AP(buf, boff, [[DC * TILE, 128], [M, 2], [1, M]])
                    return eng.dma_start(dstap, srcap).then_inc(dsem, 16)
            nd = 1
            step("dma", fl, writes=bres + gres, n_dma=nd, dcls=f"wst{p}")
            cvs = []
            for i, (nm, wb, M_, kk_, w0_, w1_, wres) in enumerate(grp):
                if nm in ("g1", "g2", "g3", "w1", "w2", "Wd", "Wy"):
                    eng_name = "act"
                else:
                    eng_name = "pool"
                if eng_name == "act":
                    def fcp(eng, wb=wb, M_=M_, kk_=kk_, w0_=w0_, w1_=w1_, buf=buf, boff=boff, i=i):
                        return eng.activation(wb[:, kk_ * M_ + w0_:kk_ * M_ + w1_],
                                              buf[:, boff + i * TILE:boff + i * TILE + (w1_ - w0_)],
                                              AF.Identity)
                else:
                    def fcp(eng, wb=wb, M_=M_, kk_=kk_, w0_=w0_, w1_=w1_, buf=buf, boff=boff, i=i):
                        return eng.tensor_copy(wb[:, kk_ * M_ + w0_:kk_ * M_ + w1_],
                                               buf[:, boff + i * TILE:boff + i * TILE + (w1_ - w0_)])
                cvs.append((eng_name, fcp, bres + gres, wres))
            pend_cv.append((p, name, wround[0], cvs))
            n -= 1

    def need_w(wname):
        while any(grp[0][0] == wname for grp, _p in bigjobs):
            drain_w(1)
        for ent in [ent for ent in list(pend_cv) if ent[1] == wname]:
            pend_cv.remove(ent)
            emit_cvs(ent[3])

    def load_x(j):
        pi = j % 2
        for h in range(2):
            def f(eng, dsem, j=j, h=h):
                src = bass.AP(xT, j * TILE + h * 4 * 128 * T,
                              [[T, 128], [128 * T, DC // 2], [1, TILE]])
                return eng.dma_start(xbuf[pi][:, h * 2048:(h + 1) * 2048],
                                     src).then_inc(dsem, 16)
            step("dma", f, writes=(f"xbuf{pi}" + ("lo", "hi")[h],),
                 n_dma=1, dcls=f"x{pi}" + ("", "h")[h])

    def xr(pi):
        return (f"xbuf{pi}lo", f"xbuf{pi}hi")

    CASTR = tuple(f"castb{c}" for c in range(8))

    def p0bufs(q):
        bx = {0: xnb[0], 1: xnb[1]}
        xres = {0: ("xnb0lo", "xnb0hi"), 1: ("xnb1lo", "xnb1hi")}
        bq = {0: castb, 1: h1b}
        qres = {0: CASTR, 1: ("h1b",)}
        if q % 2:
            return (bq, qres, bx, xres)
        return (bx, xres, bq, qres)

    def p0slots(q):
        a = (q % 2)
        pp = 64 * a
        P = {h: th4row[pp:pp + 1, h * TILE:(h + 1) * TILE] for h in range(2)}
        Q = {h: p0rows[pp:pp + 1, h * TILE:(h + 1) * TILE] for h in range(2)}
        M = {h: p0rows[pp:pp + 1, (2 + h) * TILE:(3 + h) * TILE] for h in range(2)}
        bank = {0: 4 - 2 * a, 1: 5 - 2 * a}
        bcb = {0: 6 * (1 - a), 1: 6 * (1 - a) + 1}
        sfx = str(a)
        return P, Q, M, bank, bcb, sfx

    def emit_p0_stage(q):
        bufx, xres, bufq, qres = p0bufs(q)
        for h in range(2):
            for hf in range(2):
                def fbx(eng, h=h, hf=hf, bufx=bufx):
                    return eng.tensor_copy(bufx[h][:, hf * 2048:(hf + 1) * 2048],
                                           xbuf[h][:, hf * 2048:(hf + 1) * 2048])
                if xres[h] is CASTR:
                    wr = (CASTR[hf * 4:hf * 4 + 4])
                else:
                    wr = (xres[h][hf],) if len(xres[h]) == 2 else xres[h]
                step("dve", fbx, reads=(xr(h)[hf],), writes=wr)
        for h in range(2):
            def fsq(eng, h=h, bufx=bufx, bufq=bufq):
                return eng.activation(bufq[h][:, :], bufx[h][:, :], AF.Square)
            step("act", fsq, reads=xres[h], writes=qres[h])

    def emit_p0_pair(q, drains=0):
        e_, o_ = 2 * q, 2 * q + 1
        bufx, xres, bufq, qres = p0bufs(q)
        P, Q, M, bank, bcb, sfx = p0slots(q)
        for h in range(2):
            for hf in range(2):
                def fred(eng, h=h, hf=hf):
                    ins = None
                    for c in range(hf * 4, hf * 4 + 4):
                        ins = eng.matmul(PB[bank[h]][0:1, :], onescb[:, :],
                                         bufx[h][:, c * TILE:(c + 1) * TILE],
                                         start=(c == 0), stop=(c == DC - 1))
                    return ins
                rr = (xres[h][hf],) if len(xres[h]) == 2 else xres[h]
                step("pe", fred, reads=rr + ("onescb",), writes=(f"pb{bank[h]}",))
        for h in range(2):
            def fred2(eng, h=h):
                ins = None
                for c in range(DC):
                    ins = eng.matmul(PB[bank[h]][32:33, :], onescb[:, :],
                                     bufq[h][:, c * TILE:(c + 1) * TILE],
                                     start=(c == 0), stop=(c == DC - 1))
                return ins
            step("pe", fred2, reads=qres[h] + ("onescb",), writes=(f"pb{bank[h]}",))
        drain_w(drains)
        for h in range(2):
            def fzc(eng, h=h):
                return eng.tensor_scalar(Q[h], PB[bank[h]][0:1, :], 1.0, 0.0,
                                         ALU.mult, ALU.add)
            step("dve", fzc, reads=(f"pb{bank[h]}",), writes=(f"p0Q{sfx}{h}",))
        for h in range(2):
            def fz(eng, h=h):
                return eng.scalar_tensor_tensor(P[h], Q[h], -1.0 / 1024.0,
                                                Q[h], ALU.mult, ALU.mult)
            step("dve", fz, reads=(f"p0Q{sfx}{h}",),
                 writes=(f"p0P{sfx}{h}", "th4row"), strict=True)
        for h in range(2):
            def fq(eng, h=h):
                return eng.scalar_tensor_tensor(P[h], P[h], 1.0,
                                                PB[bank[h]][32:33, :],
                                                ALU.mult, ALU.add)
            step("dve", fq, reads=(f"pb{bank[h]}", f"p0P{sfx}{h}"),
                 writes=(f"p0P{sfx}{h}", "th4row"), strict=True)
        for h in range(2):
            def fln(eng, h=h):
                return eng.activation(Q[h], P[h], AF.Ln,
                                      bias=eps_t[0:1, 0:1], scale=1.0 / 1024.0)
            step("act", fln, reads=(f"p0P{sfx}{h}", "eps_t"),
                 writes=(f"p0Q{sfx}{h}",))
        for h in range(2):
            def frr(eng, h=h):
                return eng.activation(P[h], Q[h], AF.Exp, scale=-0.5)
            step("act", frr, reads=(f"p0Q{sfx}{h}",),
                 writes=(f"p0P{sfx}{h}", "th4row"), strict=True)
        for h in range(2):
            def fmr(eng, h=h, j=(e_, o_)[h]):
                return eng.scalar_tensor_tensor(M[h], PB[bank[h]][0:1, :],
                                                1.0 / 1024.0, P[h],
                                                ALU.mult, ALU.mult,
                                                accum_out=mrs[0:1, j:j + 1])
            step("dve", fmr, reads=(f"pb{bank[h]}", f"p0P{sfx}{h}"),
                 writes=(f"p0M{sfx}{h}", "mrs0"))
        drain_w(drains)
        pp_ = 64 * (q % 2)
        for h in range(2):
            def fbc(eng, h=h, pp_=pp_):
                return eng.matmul(PB[bcb[h]][:, :], onesr[pp_:pp_ + 1, :], P[h],
                                  start=True, stop=True)
            step("pe", fbc, reads=(f"p0P{sfx}{h}", "th4row", "onesr"),
                 writes=(f"pb{bcb[h]}",))
        for h in range(2):
            def fxsum(eng, h=h, j=(e_, o_)[h]):
                ins = None
                for c in range(DC):
                    ins = eng.scalar_tensor_tensor(
                        tw0[:, :], bufx[h][:, c * TILE:(c + 1) * TILE],
                        1.0, PB[bcb[h]][:, :], ALU.mult, ALU.mult,
                        accum_out=xs_store[:, c * NT + j:c * NT + j + 1])
                return ins
            step("dve", fxsum, reads=xres[h] + (f"pb{bcb[h]}",),
                 writes=("tw0", f"xs_store{h}"))
        def fpk(eng, dsem, e_=e_, o_=o_):
            eng.dma_start(rmrp[:, e_ * 8:e_ * 8 + 4], P[0]).then_inc(dsem, 16)
            eng.dma_start(rmrp[:, e_ * 8 + 4:e_ * 8 + 8], M[0]).then_inc(dsem, 16)
            eng.dma_start(rmrp[:, o_ * 8:o_ * 8 + 4], P[1]).then_inc(dsem, 16)
            return eng.dma_start(rmrp[:, o_ * 8 + 4:o_ * 8 + 8],
                                 M[1]).then_inc(dsem, 16)
        step("dma", fpk, reads=(f"p0P{sfx}0", f"p0P{sfx}1", f"p0M{sfx}0",
                                f"p0M{sfx}1", "th4row"),
             writes=("rmrp",), n_dma=4, dcls="pack")
        drain_w(drains)

    def emit_gf1():
        def fmrs(eng):
            return eng.tensor_scalar(mrs[0:1, 0:8], mrs[0:1, 0:8], 1.0, 0.0, ALU.mult, ALU.add,
                                     accum_out=mrsR[0:1, :])
        step("dve", fmrs, reads=("mrs0",), writes=("mrs0", "mrsR"), strict=True)
        step("pe", lambda eng: eng.matmul(PB[3][:, 0:1], onesr[0:1, :], mrsR[0:1, :],
                                          start=True, stop=True),
             reads=("mrsR", "onesr"), writes=("pb3",))
        step("act", lambda eng: eng.activation(mrsb[:, :], PB[3][:, 0:1], AF.Identity),
             reads=("pb3",), writes=("mrsb",))
        def fxnm(eng):
            ins = None
            for c in range(DC):
                ins = eng.tensor_scalar(jr8[:, :], xs_store[:, c * NT:(c + 1) * NT], 1.0, 0.0,
                                        ALU.mult, ALU.add, accum_out=xnmf[:, c:c + 1])
            return ins
        step("dve", fxnm, reads=("xs_store0", "xs_store1"), writes=("jr8", "xnmf"), strict=True)
        def fxnm2(eng):
            return eng.tensor_scalar(xnmf[:, :], xnmf[:, :], mrsb[:, 0:1], 1.0 / 4096.0,
                                     ALU.subtract, ALU.mult)
        step("dve", fxnm2, reads=("xnmf", "mrsb"), writes=("xnmf",), strict=True)
        step("dve", lambda eng: eng.tensor_copy(xnmB[:, :], xnmf[:, :]),
             reads=("xnmf",), writes=("xnmB",), strict=True)
        def fm(eng):
            ins = None
            for k in range(8):
                for m in range(4):
                    ins = eng.matmul(PB[2 + m][:, 0:1],
                                     castb[:, k * 512 + m * 128:k * 512 + (m + 1) * 128],
                                     xnmB[:, k:k + 1], start=(k == 0), stop=(k == 7))
            return ins
        step("pe", fm, reads=tuple(f"castb{c}" for c in range(8)) + ("xnmB",),
             writes=("pb2", "pb3", "pb4", "pb5"))
        for m in range(4):
            def fe(eng, m=m):
                return eng.activation(hg1f[:, m:m + 1], PB[2 + m][:, 0:1], AF.Gelu,
                                      bias=bt["gb1"][:, m:m + 1])
            step("act", fe, reads=(f"pb{2 + m}", "b_gb1"), writes=("hg1f",))
        step("dve", lambda eng: eng.tensor_copy(hg1b[:, :], hg1f[:, :]),
             reads=("hg1f",), writes=("hg1b",))

    def emit_gf2():
        def fm(eng):
            ins = None
            for k in range(4):
                for m in range(2):
                    ins = eng.matmul(PB[6 + m][:, 0:1],
                                     h2b[:, k * 256 + m * 128:k * 256 + (m + 1) * 128],
                                     hg1b[:, k:k + 1], start=(k == 0), stop=(k == 3))
            return ins
        step("pe", fm, reads=("h2b", "hg1b"), writes=("pb6", "pb7"))
        for m in range(2):
            def fe(eng, m=m):
                return eng.activation(hg2f[:, m:m + 1], PB[6 + m][:, 0:1], AF.Gelu,
                                      bias=bt["gb2"][:, m:m + 1])
            step("act", fe, reads=(f"pb{6 + m}", "b_gb2"), writes=("hg2f",))
        step("dve", lambda eng: eng.tensor_copy(hg2b[:, :], hg2f[:, :]),
             reads=("hg2f",), writes=("hg2b",))

    def emit_gf3():
        def fm(eng):
            ins = None
            for k in range(2):
                ins = eng.matmul(PB[2][0:1, 0:1], g3B[:, k:k + 1], hg2b[:, k:k + 1],
                                 start=(k == 0), stop=(k == 1))
            return ins
        step("pe", fm, reads=("g3B", "hg2b"), writes=("pb2",))
        step("act", lambda eng: eng.activation(th3[0:1, :], PB[2][0:1, 0:1], AF.Tanh,
                                               scale=0.5, bias=g3h[0:1, 0:1]),
             reads=("pb2", "g3h"), writes=("th3",))
        def fc01a(eng):
            return eng.tensor_tensor(C0[0:1, :], bt["ar"][0:1, :], th3[0:1, :], ALU.mult)
        step("dve", fc01a, reads=("b_ar", "th3"), writes=("C0",))
        def fc01b(eng):
            eng.tensor_scalar(C0[0:1, :], C0[0:1, :], 0.7, 0.1, ALU.mult, ALU.add)
            return eng.tensor_scalar(C1[0:1, :], bt["ar"][0:1, :], 0.3, None, ALU.mult)
        step("dve", fc01b, reads=("b_ar", "C0"), writes=("C0", "C1"), strict=True)

    def unpack_rows(j):
        if j < 2:
            def f(eng, j=j):
                eng.tensor_copy(r_row[0:1, :], th4row[0:1, j * TILE:(j + 1) * TILE])
                return eng.tensor_copy(mr_row[0:1, :],
                                       p0rows[0:1, (2 + j) * TILE:(3 + j) * TILE])
            step("dve", f, reads=(f"p0P0{j}", f"p0M0{j}", "th4row"),
                 writes=("r_row", "mr_row"), strict=True)
            return
        def f(eng, dsem, j=j):
            eng.dma_start(r_row[0:1, :], rmrp[:, j * 8:j * 8 + 4]).then_inc(dsem, 16)
            return eng.dma_start(mr_row[0:1, :], rmrp[:, j * 8 + 4:j * 8 + 8]).then_inc(dsem, 16)
        step("dma", f, reads=("rmrp",), writes=("r_row", "mr_row"), n_dma=2, dcls="pack")

    def bcast_rows(j, evac=True):
        b1 = pbank()
        b2 = pbank()
        def f1(eng, b1=b1):
            return eng.matmul(PB[b1][:, :], onesr[0:1, :], r_row[0:1, :],
                              start=True, stop=True)
        step("pe", f1, reads=("r_row", "onesr"), writes=(f"pb{b1}",))
        def f2(eng, b2=b2):
            return eng.matmul(PB[b2][:, :], onesr[0:1, :], mr_row[0:1, :],
                              start=True, stop=True)
        step("pe", f2, reads=("mr_row", "onesr"), writes=(f"pb{b2}",))
        if not evac:
            return (b1, b2)
        def f3(eng, b1=b1):
            return eng.activation(rb_sb[:, :], PB[b1][:, :], AF.Identity)
        step("act", f3, reads=(f"pb{b1}",), writes=("rb_sb",))
        def f4(eng, b2=b2):
            return eng.activation(mrb_sb[:, :], PB[b2][:, :], AF.Identity)
        step("act", f4, reads=(f"pb{b2}",), writes=("mrb_sb",))
        return None

    def emit_xn(j, banks=None):
        pi = j % 2
        if banks is not None:
            b1, b2 = banks
            rbap, mbap = PB[b1][:, :], PB[b2][:, :]
            rres = (f"pb{b1}", f"pb{b2}")
        else:
            rbap, mbap = rb_sb[:, :], mrb_sb[:, :]
            rres = ("rb_sb", "mrb_sb")
        for c in range(DC):
            def f(eng, pi=pi, c=c, rbap=rbap, mbap=mbap):
                eng.tensor_tensor(thT[:, 0:TILE], xbuf[pi][:, c * TILE:(c + 1) * TILE],
                                  rbap, ALU.mult)
                return eng.scalar_tensor_tensor(state[pi][:, c * TILE:(c + 1) * TILE],
                                                thT[:, 0:TILE], 1.0, mbap,
                                                ALU.mult, ALU.subtract)
            step("dve", f, reads=(xr(pi)[0], xr(pi)[1]) + rres,
                 writes=("thT", stchunk(pi, c)))

    def emit_xnb(j, eng_name="pool"):
        pi = j % 2
        for h in range(2):
            if eng_name == "act":
                def f(eng, pi=pi, h=h):
                    return eng.activation(xnb[pi][:, h * 2048:(h + 1) * 2048],
                                          state[pi][:, h * 2048:(h + 1) * 2048],
                                          AF.Identity)
            else:
                def f(eng, pi=pi, h=h):
                    return eng.tensor_copy(xnb[pi][:, h * 2048:(h + 1) * 2048],
                                           state[pi][:, h * 2048:(h + 1) * 2048])
            step(eng_name, f, reads=tuple(stchunk(pi, c) for c in range(h * 4, h * 4 + 4)),
                 writes=(f"xnb{pi}" + ("lo" if h == 0 else "hi",)[0],))

    HB = [2, 3, 0]
    hrot = [0]

    def hbank():
        hrot[0] = (hrot[0] + 1) % 3
        return HB[hrot[0]]

    PBK = [1, 5, 6, 7]
    prot = [0]

    def pbank():
        prot[0] = (prot[0] + 1) % 4
        return PBK[prot[0]]

    castms = {}
    h1ev = {}
    h2ev = {}

    def emit_h1(j, k, ms):
        pi = j % 2
        if k == 1:
            rhs, rres = xnb[pi], (f"xnb{pi}lo", f"xnb{pi}hi")
            korder = tuple(range(8))
        else:
            rhs, rres = castb, tuple(f"castb{c}" for c in range(8))
            korder = (0, 4, 1, 5, 2, 6, 3, 7)
        for m in ms:
            b = hbank()
            if k > 1 and m == 0:
                cw = dict(castms)
                def f(eng, m=m, b=b, rhs=rhs, cw=cw, korder=korder):
                    ins = None
                    for i, kk in enumerate(korder):
                        we, wv = cw[kk]
                        eng.wait_ge(sem[we], wv)
                        ins = eng.matmul(PB[b][:, :],
                                         W1B[:, kk * 1024 + m * 128:kk * 1024 + (m + 1) * 128],
                                         rhs[:, kk * TILE:(kk + 1) * TILE],
                                         start=(i == 0), stop=(i == 7))
                    return ins
                step("pe", f, reads=("Ww1",), lazy_reads=rres, writes=(f"pb{b}",))
            else:
                def f(eng, m=m, b=b, rhs=rhs, korder=korder):
                    ins = None
                    for i, kk in enumerate(korder):
                        ins = eng.matmul(PB[b][:, :],
                                         W1B[:, kk * 1024 + m * 128:kk * 1024 + (m + 1) * 128],
                                         rhs[:, kk * TILE:(kk + 1) * TILE],
                                         start=(i == 0), stop=(i == 7))
                    return ins
                step("pe", f, reads=rres + ("Ww1",), writes=(f"pb{b}",))
            def fe(eng, m=m, b=b):
                return eng.activation(h1b[:, m * TILE:(m + 1) * TILE], PB[b][:, :], AF.Gelu,
                                      bias=bt["b1c"][:, m:m + 1])
            step("act", fe, reads=(f"pb{b}", "b_b1c"), writes=("h1b",))
            h1ev[m] = count["act"]

    def emit_h23z(j, k, defer_cb=False):
        t4off = (j % 2) * TILE if defer_cb else 0
        for m in range(4):
            b = hbank()
            if m == 0:
                ev = dict(h1ev)
                def f(eng, m=m, b=b, ev=ev):
                    ins = None
                    for kk in range(8):
                        eng.wait_ge(sem["act"], ev[kk])
                        ins = eng.matmul(PB[b][:, :],
                                         W2B[:, kk * 512 + m * 128:kk * 512 + (m + 1) * 128],
                                         h1b[:, kk * TILE:(kk + 1) * TILE],
                                         start=(kk == 0), stop=(kk == 7))
                    return ins
                step("pe", f, reads=("Ww2",), lazy_reads=("h1b",), writes=(f"pb{b}",))
            else:
                def f(eng, m=m, b=b):
                    ins = None
                    for kk in range(8):
                        ins = eng.matmul(PB[b][:, :],
                                         W2B[:, kk * 512 + m * 128:kk * 512 + (m + 1) * 128],
                                         h1b[:, kk * TILE:(kk + 1) * TILE],
                                         start=(kk == 0), stop=(kk == 7))
                    return ins
                step("pe", f, reads=("h1b", "Ww2"), writes=(f"pb{b}",))
            def fe(eng, m=m, b=b):
                return eng.activation(h2b[:, m * TILE:(m + 1) * TILE], PB[b][:, :], AF.Gelu,
                                      bias=bt["b2c"][:, m:m + 1])
            step("act", fe, reads=(f"pb{b}", "b_b2c"), writes=("h2b",))
            h2ev[m] = count["act"]
        for m in range(2):
            b = hbank()
            if m == 0:
                ev = dict(h2ev)
                def f(eng, m=m, b=b, ev=ev):
                    ins = None
                    for kk in range(4):
                        eng.wait_ge(sem["act"], ev[kk])
                        ins = eng.matmul(PB[b][:, :],
                                         W3B[:, kk * 256 + m * 128:kk * 256 + (m + 1) * 128],
                                         h2b[:, kk * TILE:(kk + 1) * TILE],
                                         start=(kk == 0), stop=(kk == 3))
                    return ins
                step("pe", f, reads=("Ww3",), lazy_reads=("h2b",), writes=(f"pb{b}",))
            else:
                def f(eng, m=m, b=b):
                    ins = None
                    for kk in range(4):
                        ins = eng.matmul(PB[b][:, :],
                                         W3B[:, kk * 256 + m * 128:kk * 256 + (m + 1) * 128],
                                         h2b[:, kk * TILE:(kk + 1) * TILE],
                                         start=(kk == 0), stop=(kk == 3))
                    return ins
                step("pe", f, reads=("h2b", "Ww3"), writes=(f"pb{b}",))
            def fe(eng, m=m, b=b):
                return eng.activation(h1b[:, m * TILE:(m + 1) * TILE], PB[b][:, :], AF.Gelu,
                                      bias=bt["b3c"][:, m:m + 1])
            step("act", fe, reads=(f"pb{b}", "b_b3c"), writes=("h1b",))
            h2ev["h3%d" % m] = count["act"]
        b = hbank()
        ev0 = h2ev["h30"]
        ev1 = h2ev["h31"]
        def fz4(eng, b=b, ev0=ev0, ev1=ev1):
            ins = None
            for kk in range(2):
                eng.wait_ge(sem["act"], ev0 if kk == 0 else ev1)
                ins = eng.matmul(PB[b][0:1, :], W4B[:, kk:kk + 1],
                                 h1b[:, kk * TILE:(kk + 1) * TILE],
                                 start=(kk == 0), stop=(kk == 1))
            return ins
        step("pe", fz4, reads=("Ww4",), lazy_reads=("h1b",), writes=(f"pb{b}",))
        def fth4(eng, b=b, t4off=t4off):
            return eng.activation(th4row[0:1, t4off:t4off + TILE], PB[b][0:1, :], AF.Tanh,
                                  scale=0.5, bias=b4h[0:1, 0:1])
        step("act", fth4, reads=(f"pb{b}", "b4h"), writes=("th4row",))
        if not defer_cb:
            emit_fcb(0)

    def emit_fcb(t4off):
        def fcb(eng, t4off=t4off):
            return eng.activation(cbrow[0:1, :], th4row[0:1, t4off:t4off + TILE],
                                  AF.Identity, scale=C1[0:1, 0:1], bias=C0[0:1, 0:1])
        step("act", fcb, reads=("th4row", "C1", "C0"), writes=("cbrow",), strict=True)

    def emit_cb(j, k):
        def fbc(eng):
            return eng.matmul(PB[4][:, :], onesr[0:1, :], cbrow[0:1, :], start=True, stop=True)
        step("pe", fbc, reads=("cbrow", "onesr"), writes=("pb4",))
        return 4

    def emit_upd_pair(j, cb, c):
        pi = j % 2
        sA = ALU.add if c < 2 else ALU.subtract
        sB = ALU.subtract if c < 2 else ALU.add
        def f(eng, pi=pi, c=c, sA=sA, sB=sB, cb=cb):
            cA = state[pi][:, c * TILE:(c + 1) * TILE]
            cB = state[pi][:, (c + 4) * TILE:(c + 5) * TILE]
            eng.tensor_tensor(tw0[:, :], PB[cb][:, :], cB, ALU.mult)
            eng.tensor_tensor(tw1[:, :], PB[cb][:, :], cA, ALU.mult)
            eng.tensor_tensor(cA, cA, tw0[:, :], sA)
            return eng.tensor_tensor(cB, cB, tw1[:, :], sB)
        step("dve", f, reads=(f"pb{cb}", stchunk(pi, c), stchunk(pi, c + 4)),
             writes=(stchunk(pi, c), stchunk(pi, c + 4), "tw0", "tw1"))

    def emit_upd_casts(j, cb, weave=None):
        weave = list(weave or [])
        pi = j % 2
        def fp0a(eng, pi=pi, cb=cb):
            cA = state[pi][:, 0:TILE]
            cB = state[pi][:, 4 * TILE:5 * TILE]
            eng.tensor_tensor(tw0[:, :], PB[cb][:, :], cB, ALU.mult)
            eng.tensor_tensor(tw1[:, :], PB[cb][:, :], cA, ALU.mult)
            eng.tensor_tensor(cA, cA, tw0[:, :], ALU.add)
            return eng.tensor_copy(castb[:, 0:TILE], cA)
        step("dve", fp0a, reads=(f"pb{cb}", stchunk(pi, 0), stchunk(pi, 4)),
             writes=(stchunk(pi, 0), "tw0", "tw1", "castb0"))
        castms[0] = ("dve", count["dve"])
        def fp0b(eng, pi=pi):
            cB = state[pi][:, 4 * TILE:5 * TILE]
            eng.tensor_tensor(cB, cB, tw1[:, :], ALU.subtract)
            return eng.tensor_copy(castb[:, 4 * TILE:5 * TILE], cB)
        step("dve", fp0b, reads=("tw1", stchunk(pi, 4)),
             writes=(stchunk(pi, 4), "castb4"))
        castms[4] = ("dve", count["dve"])
        for c in range(1, 4):
            emit_upd_pair(j, cb, c)
            for cc in (c, c + 4):
                def fcst(eng, pi=pi, cc=cc):
                    return eng.activation(castb[:, cc * TILE:(cc + 1) * TILE],
                                          state[pi][:, cc * TILE:(cc + 1) * TILE],
                                          AF.Identity)
                step("act", fcst, reads=(stchunk(pi, cc),), writes=(f"castb{cc}",))
                castms[cc] = ("act", count["act"])
                if weave:
                    emit_proj_evac(*weave.pop(0))
        while weave:
            emit_proj_evac(*weave.pop(0))

    def emit_proj_mm(j, which):
        pi = j % 2
        wmap = {"a": (WaB, "WWa"), "b": (WbB, "WWb"), "c": (WcB, "WWc")}
        WB, wres = wmap[which]
        out = []
        for n in range(NC):
            b = pbank()
            def f(eng, n=n, b=b, pi=pi, WB=WB):
                ins = None
                for kk in range(8):
                    ins = eng.matmul(PB[b][:, :],
                                     WB[:, kk * 256 + n * 128:kk * 256 + (n + 1) * 128],
                                     xnb[pi][:, kk * TILE:(kk + 1) * TILE],
                                     start=(kk == 0), stop=(kk == 7))
                return ins
            step("pe", f, reads=(f"xnb{pi}lo", f"xnb{pi}hi", wres), writes=(f"pb{b}",))
            out.append((which, n, b))
        return out

    def emit_proj_evac(which, n, b):
        if which == "a":
            def fe(eng, n=n, b=b):
                return eng.activation(thT[:, n * TILE:(n + 1) * TILE], PB[b][:, :], AF.Tanh,
                                      scale=0.5, bias=ba_h[:, n:n + 1])
            step("act", fe, reads=(f"pb{b}", "ba_h"), writes=("thT",))
        elif which == "b":
            def fe(eng, n=n, b=b):
                return eng.activation(buT[:, n * TILE:(n + 1) * TILE], PB[b][:, :],
                                      AF.Identity, bias=bt["bb2"][:, n:n + 1])
            step("act", fe, reads=(f"pb{b}", "b_bb2"), writes=("buT",))
        else:
            def fe(eng, n=n, b=b):
                return eng.activation(cT[:, n * TILE:(n + 1) * TILE], PB[b][:, :], AF.Tanh,
                                      bias=bt["bc2"][:, n:n + 1])
            step("act", fe, reads=(f"pb{b}", "b_bc2"), writes=("cT",))

    def emit_proj(j, which):
        for which_, n, b in emit_proj_mm(j, which):
            emit_proj_evac(which_, n, b)

    def emit_scan(j):
        def f1(eng):
            eng.tensor_scalar(aT[:, :], thT[:, :], 0.5, 0.5, ALU.mult, ALU.add)
            return eng.tensor_scalar(thT[:, :], thT[:, :], -0.5, 0.5, ALU.mult, ALU.add)
        step("dve", f1, reads=("thT",), writes=("aT", "thT"))
        def f2(eng):
            return eng.tensor_tensor(buT[:, :], thT[:, :], buT[:, :], ALU.mult)
        step("dve", f2, reads=("thT", "buT"), writes=("buT",))
        for n in range(NC):
            def f3(eng, n=n):
                return eng.tensor_tensor_scan(buT[:, n * TILE:(n + 1) * TILE],
                                              aT[:, n * TILE:(n + 1) * TILE],
                                              buT[:, n * TILE:(n + 1) * TILE],
                                              carry[:, n:n + 1], ALU.mult, ALU.add)
            step("dve", f3, reads=("aT", "buT", "carry"), writes=("buT",))
        def f4(eng):
            ins = None
            for n in range(NC):
                ins = eng.tensor_copy(carry[:, n:n + 1],
                                      buT[:, n * TILE + TILE - 1:n * TILE + TILE])
            return ins
        step("dve", f4, reads=("buT",), writes=("carry",), strict=True)

    def emit_cs(j):
        def f(eng):
            return eng.tensor_tensor(csT[:, :], cT[:, :], buT[:, :], ALU.mult)
        step("dve", f, reads=("cT", "buT"), writes=("csT",))

    ybank = {}

    def emit_ygroup(j, m):
        pi = j % 2
        b = pbank()
        ybank[(j, m)] = b
        def fy(eng, m=m, b=b, pi=pi):
            ins = None
            for kk in range(8):
                ins = eng.matmul(PB[b][:, :],
                                 WdB[:, kk * 1024 + m * 128:kk * 1024 + (m + 1) * 128],
                                 xnb[pi][:, kk * TILE:(kk + 1) * TILE],
                                 start=(kk == 0), stop=False)
            for kk in range(2):
                ins = eng.matmul(PB[b][:, :],
                                 WyB[:, kk * 1024 + m * 128:kk * 1024 + (m + 1) * 128],
                                 csT[:, kk * TILE:(kk + 1) * TILE],
                                 start=False, stop=(kk == 1))
            return ins
        step("pe", fy, reads=(f"xnb{pi}lo", f"xnb{pi}hi", "csT", "WWd", "WWy"),
             writes=(f"pb{b}",))

    def emit_pfold(j, m):
        pi = j % 2
        b = ybank[(j, m)]
        def fp(eng, m=m, b=b):
            return eng.scalar_tensor_tensor(gbuf[:, m * TILE:(m + 1) * TILE], PB[b][:, :],
                                            byd[:, m:m + 1],
                                            gbuf[:, m * TILE:(m + 1) * TILE],
                                            ALU.add, ALU.mult)
        step("dve", fp, reads=(f"pb{b}", "byd", f"g{m}"), writes=(f"g{m}",))
        def fo(eng, c=m, pi=pi):
            return eng.tensor_tensor(xbuf[pi][:, c * TILE:(c + 1) * TILE],
                                     gbuf[:, c * TILE:(c + 1) * TILE],
                                     xbuf[pi][:, c * TILE:(c + 1) * TILE], ALU.add)
        step("dve", fo, reads=(f"g{m}", xr(pi)[m // 4]), writes=(xr(pi)[m // 4],))

    def emit_silu_chunk(j, cc):
        pi = j % 2
        def fs(eng, pi=pi, cc=cc):
            return eng.activation(gbuf[:, cc * TILE:(cc + 1) * TILE],
                                  state[pi][:, cc * TILE:(cc + 1) * TILE], AF.Silu)
        step("act", fs, reads=(stchunk(pi, cc),), writes=(f"g{cc}",))

    def emit_outdma(j, h):
        pi = j % 2
        def fo(eng, dsem, pi=pi, j=j, h=h):
            dst = bass.AP(oT, j * TILE + h * 4 * 128 * T,
                          [[T, 128], [128 * T, DC // 2], [1, TILE]])
            return eng.dma_start(dst, xbuf[pi][:, h * 2048:(h + 1) * 2048],
                                 ).then_inc(dsem, 16)
        step("dma", fo, reads=(xr(pi)[h],), n_dma=1, dcls="out")

    load_x(0)
    load_x(1)
    emit_p0_stage(0)
    step("dma", fbias, writes=tuple("b_" + n for n in db), n_dma=len(db), dcls="bias")
    drain_w(2)
    emit_p0_pair(0, drains=2)
    step("dve", bias_prep,
         reads=("b_ba2", "b_b4", "b_gb3", "b_bd8", "b_by8"),
         writes=("ba_h", "b4h", "g3h", "byd"))
    unpack_rows(0)
    bk0 = bcast_rows(0, evac=False)
    emit_xn(0, banks=bk0)
    if debug:
        dbg("xn0", state[0][:, :], [stchunk(0, c) for c in range(8)])
    load_x(2)
    unpack_rows(1)
    bk1 = bcast_rows(1, evac=False)
    emit_xn(1, banks=bk1)
    load_x(3)
    emit_p0_stage(1)
    load_x(4)
    load_x(5)
    emit_p0_pair(1, drains=2)
    emit_p0_stage(2)
    load_x(6)
    load_x(7)
    emit_p0_pair(2, drains=2)
    emit_p0_stage(3)
    emit_p0_pair(3, drains=2)
    if debug:
        dbg("rmrp", rmrp[:, :], ["rmrp"])
        dbg("mrs", mrs[0:1, :], ["mrs0"])
    emit_xnb(0, "act")
    drain_w(4)
    need_w("w1")
    emit_h1(0, 1, range(8))
    allow_g[0] = True
    need_w("g1")
    need_w("w2")
    need_w("w3")
    need_w("w4")
    emit_h23z(0, 1, defer_cb=True)
    emit_xnb(1, "act")
    need_w("g2")
    need_w("g3")
    if debug:
        dbg("th4row0", th4row[0:1, 0:TILE], ["th4row"])
    need_w("Wa")
    emit_proj(0, "a")
    need_w("Wb")
    if debug:
        dbg("WbEarly", WbB[:, :], ["WWb"])
    emit_proj(0, "b")
    need_w("Wc")
    emit_proj(0, "c")
    if debug:
        dbg("thT0", thT[:, :], ["thT"])
        dbg("buT0", buT[:, :], ["buT"])
    drain_w(10 ** 9)
    drain_cv(10 ** 9, force=True)
    load_x(0)
    load_x(1)
    if debug:
        dbg("WaBd", WaB[:, :], ["WWa"])
        dbg("WbBd", WbB[:, :], ["WWb"])
        dbg("WcBd", WcB[:, :], ["WWc"])
        dbg("W4Bd", W4B[:, :], ["Ww4"])
        dbg("W1Bd", W1B[:, :], ["Ww1"])
        dbg("g1Bd", castb[:, :], list(CASTR))
        dbg("g2Bd", h2b[:, 0:1024], ["h2b"])
        dbg("g3Bd", g3B[:, :], ["g3B"])

    emit_gf1()
    if debug:
        dbg("hg1f", hg1f[:, :], ["hg1f"])
    emit_gf2()
    if debug:
        dbg("hg2f", hg2f[:, :], ["hg2f"])
    emit_gf3()
    if debug:
        dbg("xnmf", xnmf[:, :], ["xnmf"])
        dbg("C0", C0[0:1, :], ["C0"])
        dbg("C1", C1[0:1, :], ["C1"])
        dbg("th3", th3[0:1, :], ["th3"])

    def emit_tile_ffn_tail(j, weave_next, lead_full=True):
        emit_h1(j, 2, range(8))
        emit_h23z(j, 2)
        if weave_next and lead_full:
            bcast_rows(j + 1)
        emit_ygroup(j, 4)
        cb = emit_cb(j, 2)
        emit_upd_casts(j, cb)
        if weave_next and lead_full:
            emit_xn(j + 1)
        emit_h1(j, 3, range(8))
        emit_h23z(j, 3)
        emit_ygroup(j, 0)
        if j == 0 and debug:
            dbg("state0c2", state[0][:, :], [stchunk(0, c) for c in range(8)])
        cb = emit_cb(j, 3)
        emit_upd_pair(j, cb, 0)
        emit_silu_chunk(j, 0)
        emit_silu_chunk(j, 4)
        emit_ygroup(j, 1)
        emit_pfold(j, 0)
        emit_pfold(j, 4)
        if weave_next:
            if lead_full:
                emit_xnb(j + 1)
            emit_h1(j + 1, 1, range(0, 4))
        emit_upd_pair(j, cb, 1)
        emit_silu_chunk(j, 1)
        emit_silu_chunk(j, 5)
        emit_ygroup(j, 5)
        emit_pfold(j, 1)
        emit_pfold(j, 5)
        if weave_next:
            emit_h1(j + 1, 1, range(4, 8))
        emit_upd_pair(j, cb, 2)
        emit_silu_chunk(j, 2)
        emit_silu_chunk(j, 6)
        emit_ygroup(j, 2)
        emit_pfold(j, 2)
        emit_upd_pair(j, cb, 3)
        emit_silu_chunk(j, 3)
        emit_silu_chunk(j, 7)
        emit_ygroup(j, 6)
        emit_pfold(j, 6)
        emit_ygroup(j, 3)
        emit_pfold(j, 3)
        emit_outdma(j, 0)
        emit_ygroup(j, 7)
        emit_pfold(j, 7)
        if j == 0 and debug:
            dbg("p0", gbuf[:, :], [f"g{c}" for c in range(8)])
            dbg("out0", xbuf[0][:, :], ["xbuf0lo", "xbuf0hi"])
        emit_outdma(j, 1)

    emit_scan(0)
    emit_cs(0)
    if debug:
        dbg("aT0", aT[:, :], ["aT"])
        dbg("s0", buT[:, :], ["buT"])
        dbg("cT0", cT[:, :], ["cT"])
        dbg("cs0", csT[:, :], ["csT"])

    emit_fcb(0)
    cb = emit_cb(0, 1)
    emit_upd_casts(0, cb)
    if debug:
        dbg("state0c1", state[0][:, :], [stchunk(0, c) for c in range(8)])
    emit_tile_ffn_tail(0, weave_next=True, lead_full=False)
    load_x(2)
    unpack_rows(2)

    for j in range(1, NT):
        emit_h23z(j, 1)
        emit_proj(j, "a")
        cb = emit_cb(j, 1)
        emit_proj(j, "b")
        emit_proj(j, "c")
        emit_upd_casts(j, cb)
        emit_scan(j)
        emit_cs(j)
        emit_tile_ffn_tail(j, weave_next=(j + 1 < NT))
        if j + 2 < NT:
            load_x(j + 2)
            unpack_rows(j + 2)

    per_eng = {k: [] for k in ["pe", "act", "dve", "pool", "dma"]}
    for s in steps:
        per_eng[s["eng"]].append(s)

    def emit_for(eng_name):
        def run(engobj):
            maxw = {}
            for s in per_eng[eng_name]:
                for d in sorted(s["deps"]):
                    ps = steps[d]
                    if (ps["eng"] == eng_name and eng_name != "dma"
                            and not ps["is_dma"] and not s["strict"]):
                        continue
                    k = ps["key"]
                    val = ps["val"]
                    if maxw.get(k, -1) >= val:
                        continue
                    engobj.wait_ge(sem[k], val)
                    maxw[k] = val
                if s["is_dma"]:
                    s["fn"](engobj, sem[s["key"]])
                else:
                    last = s["fn"](engobj)
                    last.then_inc(sem[s["eng"]], 1)
            if eng_name == "dma":
                for k in DCLS:
                    if count[k]:
                        engobj.wait_ge(sem[k], count[k])
        return run

    with nc.Block() as block:
        @block.tensor
        def _(tensor):
            emit_for("pe")(tensor)

        @block.scalar
        def _(scalar):
            emit_for("act")(scalar)

        @block.vector
        def _(vector):
            emit_for("dve")(vector)

        @block.gpsimd
        def _(gpsimd):
            emit_for("pool")(gpsimd)

        @block.sync
        def _(sync):
            emit_for("dma")(sync)

    ctx.close()
    nc._dbg = dbg_tensors
    return nc


def _cvec(v, ncols):
    return np.ascontiguousarray(np.asarray(v, np.float32).reshape(ncols, 128).T)


def kernel(**inputs):
    g = lambda k: np.asarray(inputs[k], np.float32)
    x = g("L0_attn_x")
    base = {
        "Wa": g("L0_attn_Wa"), "Wb": g("L0_attn_Wb"), "Wc": g("L0_attn_Wc"),
        "Wd": g("L0_attn_Wd"), "Wy": g("L0_attn_Wy"),
        "w1": g("L0_ffn_cn_w1"), "w2": g("L0_ffn_cn_w2"), "w3": g("L0_ffn_cn_w3"),
        "w4": g("L0_ffn_cn_w4"),
        "g1": g("L0_ffn_gc_w1"), "g2": g("L0_ffn_gc_w2"), "g3": g("L0_ffn_gc_w3"),
        "ba2": _cvec(g("L0_attn_ba"), 2), "bb2": _cvec(g("L0_attn_bb"), 2),
        "bc2": _cvec(g("L0_attn_bc"), 2),
        "bd8": _cvec(g("L0_attn_bd"), 8), "by8": _cvec(g("L0_attn_by"), 8),
        "b1c": _cvec(g("L0_ffn_cn_b1"), 8), "b2c": _cvec(g("L0_ffn_cn_b2"), 4),
        "b3c": _cvec(g("L0_ffn_cn_b3"), 2),
        "b4": g("L0_ffn_cn_b4").reshape(1, 1),
        "gb1": _cvec(g("L0_ffn_gc_b1"), 4), "gb2": _cvec(g("L0_ffn_gc_b2"), 2),
        "gb3": g("L0_ffn_gc_b3").reshape(1, 1),
        "ar": g("L0_ffn_adaptive_range").reshape(1, 1),
    }
    base = {k: np.ascontiguousarray(v) for k, v in base.items()}
    in_maps = []
    for b in range(8):
        m = dict(base)
        m["xT"] = np.ascontiguousarray(x[b].T)
        in_maps.append(m)
    if "nc" not in _NC_CACHE:
        _NC_CACHE["nc"] = build_nc()
    res = run_bass_kernel_spmd(_NC_CACHE["nc"], in_maps, core_ids=list(range(8)))
    out = np.stack([np.asarray(res.results[b]["oT"]).T for b in range(8)])
    return np.ascontiguousarray(out.astype(np.float32))


if __name__ == "__main__":
    nc = build_nc()
    print("built ok")
